# version 05
# baseline (speedup 1.0000x reference)
import numpy as np
from contextlib import ExitStack

import concourse.bass as bass
import concourse.mybir as mybir
from concourse.bass_utils import run_bass_kernel_spmd

F32 = mybir.dt.float32
BF16 = mybir.dt.bfloat16
I32 = mybir.dt.int32
AF = mybir.ActivationFunctionType
AL = mybir.AluOpType
AX = mybir.AxisListType

B, T, HID = 2, 8192, 2048
TOKS = B * T
NCORES = 8
TPC = TOKS // NCORES
NKH, NVH, DK, DV, KS = 8, 16, 64, 128, 4
KD, VD = NKH * DK, NVH * DV
CD = 2 * KD + VD
TB = 512
NTB = T // TB
SCALE = DK ** -0.5
INTER = 5632
NIT = INTER // 128
NSEG = 4
SEG = NIT // NSEG
HALF = 1024
NHALF = TPC // HALF
NTT3 = HALF // 128

ENGINES = ("sync", "tensor", "scalar", "vector", "gpsimd")


class Rec:
    def __init__(self):
        self.streams = {e: [] for e in ENGINES}
        self.cnt = {}
        self.seq = 0

    def _app(self, eng, fn):
        self.streams[eng].append((self.seq, fn))
        self.seq += 1

    def sem(self, name):
        self.cnt[name] = 0

    def emit(self, eng, fn, inc=None, inc_n=1):
        if inc is not None:
            self.cnt[inc] += inc_n
            self._app(eng, lambda e, S, fn=fn, inc=inc, n=inc_n: fn(e).then_inc(S[inc], n))
        else:
            self._app(eng, lambda e, S, fn=fn: fn(e))

    def raw(self, eng, fn):
        self._app(eng, fn)

    def wait(self, eng, sname, val=None):
        v = self.cnt[sname] if val is None else val
        if v > 0:
            self._app(eng, lambda e, S, sname=sname, v=v: e.wait_ge(S[sname], v))


def TLOAD(e, dst_sb, src_ap, a):
    return e.dma_start(out=dst_sb[:].rearrange("p (a c) -> p a c", a=a),
                       in_=src_ap.rearrange("(a p) c -> p a c", p=128))


def COPY(e, out, in_):
    if hasattr(e, "tensor_copy"):
        return e.tensor_copy(out, in_)
    return e.copy(out, in_)


def build_program(debug_chunk=None, trunc=None, hw_debug=False):
    nc = bass.Bass("TRN2", target_bir_lowering=False, num_devices=NCORES,
                   detect_race_conditions=False)
    r = Rec()
    for s in ("ld", "ldg", "lnld", "lnst", "cc", "lnT", "gst", "gmv", "p3w", "p3g",
              "p3x", "o1st", "yst", "mm", "ve", "se", "dbgs"):
        r.sem(s)

    di = lambda n, shp, dt=F32: nc.dram_tensor(n, shp, dt, kind="ExternalInput").ap()
    x_sl = di("x_sl", [TPC, HID])
    wqkvT = di("wqkvT", [HID, 384])
    wzbaT = di("wzbaT", [HID, 260])
    cw_in = di("cw_in", [128, 12])
    tiny_in = di("tiny_in", [1, 4])
    nw_in = di("nw_in", [1, DV])
    eye_in = di("eye_in", [128, 128])
    mubn_in = di("mubn_in", [128, 128])
    mube_in = di("mube_in", [128, 128])
    blko_in = di("blko_in", [128, 128])
    sel64_in = di("sel64_in", [128, 64])
    cid_in = di("cid_in", [1, 1], I32)
    woutT_in = di("woutT_in", [HID, HID])
    gwT_in = di("gwT_in", [HID, INTER])
    uwT_in = di("uwT_in", [HID, INTER])
    dwT_in = di("dwT_in", [INTER, HID])
    gs_in = di("gs_in", [128, NIT])
    us_in = di("us_in", [128, NIT])
    ds_in = di("ds_in", [1, HID])

    y_out = nc.dram_tensor("y_out", [TPC, HID], F32, kind="ExternalOutput").ap()

    ln_sl_d = nc.dram_tensor("ln_sl_d", [TPC, HID], BF16).ap()
    ln_full_d = nc.dram_tensor("ln_full_d", [TOKS, HID], BF16, addr_space="Shared").ap()
    g_sl_d = nc.dram_tensor("g_sl_d", [TOKS, 256], BF16).ap()
    g_full_d = nc.dram_tensor("g_full_d", [NCORES * TOKS, 256], BF16, addr_space="Shared").ap()
    g_mine_d = nc.dram_tensor("g_mine_d", [TPC, HID], BF16).ap()
    out1_d = nc.dram_tensor("out1_d", [TPC, HID], F32).ap()
    hwdump = {}
    if hw_debug:
        for nm, src_shape, dt in (("lnsl", [TPC, HID], BF16), ("gsl", [TOKS, 256], BF16),
                                  ("gmine", [TPC, HID], BF16), ("o1", [TPC, HID], F32),
                                  ("lnfull", [TOKS, HID], BF16), ("negA", [128, 2], F32),
                                  ("dtbb", [128, 2], F32), ("nw2", [128, 256], F32)):
            hwdump[nm] = nc.dram_tensor("hwd_" + nm, src_shape, dt, kind="ExternalOutput").ap()
    dbg = {}
    if debug_chunk is not None:
        for nm, shp, dt in (("qkn", [128, 512], BF16), ("kn0", [64, 512], BF16),
                            ("v0", [128, 512], BF16), ("v1", [128, 512], BF16),
                            ("pc4", [128, 4], F32), ("bcr", [1, 512], F32),
                            ("nt0", [128, 128], BF16), ("nt1", [128, 128], BF16),
                            ("nn0", [128, 128], BF16), ("nn1", [128, 128], BF16),
                            ("ap0", [128, 128], BF16), ("ap1", [128, 128], BF16),
                            ("r0", [128, 192], BF16), ("r1", [128, 192], BF16),
                            ("yb0", [128, 192], BF16), ("yb1", [128, 192], BF16),
                            ("qpt0", [64, 128], BF16), ("qpt1", [64, 128], BF16),
                            ("pt0", [64, 64], BF16), ("pt1", [64, 64], BF16),
                            ("kp0", [128, 64], BF16), ("kp1", [128, 64], BF16),
                            ("sin", [64, 256], BF16), ("sout", [64, 256], BF16),
                            ("df0", [128, 128], F32), ("df1", [128, 128], F32),
                            ("gcol", [128, 2], F32), ("tvm", [128, 4], F32)):
            dbg[nm] = nc.dram_tensor("dbg_" + nm, shp, dt).ap()
        dbg_out = {nm: nc.dram_tensor("dbgo_" + nm, list(dbg[nm].tensor.shape),
                                      dbg[nm].tensor.dtype, kind="ExternalOutput").ap()
                   for nm in dbg}

    ctx = ExitStack()
    SB = lambda n, shp, dt: ctx.enter_context(nc.sbuf_tensor(n, shp, dt))

    eye_f = SB("eye_f", [128, 128], F32)
    eye_bf = SB("eye_bf", [128, 128], BF16)
    mubn = SB("mubn", [128, 128], F32)
    mube = SB("mube", [128, 128], F32)
    ones1_f = SB("ones1_f", [1, 128], F32)
    onesb = SB("onesb", [64, 64], BF16)
    cw_sb = SB("cw_sb", [128, 12], F32)
    diag_bf = SB("diag_bf", [128, 12 * 128], BF16)
    tiny_sb = SB("tiny_sb", [1, 4], F32)
    nw_sb = SB("nw_sb", [1, DV], F32)
    dtb_bc = SB("dtb_bc", [128, 2], F32)
    negA_bc = SB("negA_bc", [128, 2], F32)
    nw2_bc = SB("nw2_bc", [128, 256], F32)
    gs_t = SB("gs_t", [128, NIT], F32)
    us_t = SB("us_t", [128, NIT], F32)
    ds_bc = SB("ds_bc", [128, HID], F32)
    cid_sb = SB("cid_sb", [1, 1], I32)
    cst = SB("cst", [128, 4], F32)
    wqkv_sb = SB("wqkv_sb", [128, 16 * 384], BF16)
    wzba_sb = SB("wzba_sb", [128, 16 * 260], BF16)
    blko_bf = SB("blko_bf", [128, 128], BF16)
    sel64_bf = SB("sel64_bf", [128, 64], BF16)
    eps_col = SB("eps_col", [128, 1], F32)

    pC0 = ctx.enter_context(nc.psum_tensor("pC0", [128, 512], F32))

    p0 = ExitStack()
    S0 = lambda n, shp, dt: p0.enter_context(nc.sbuf_tensor(n, shp, dt))
    ds_row = S0("ds_row", [1, HID], F32)
    x0_sb = [S0(f"x0_{i}", [128, HID], F32) for i in range(2)]
    junk0 = S0("junk0", [128, HID], F32)
    lnbf_sb = S0("lnbf", [128, HID], BF16)
    st0 = S0("st0", [128, 8], F32)

    r.emit("gpsimd", lambda e: e.dma_start(out=eye_bf[:], in_=eye_in), "ldg", 16)
    r.emit("gpsimd", lambda e: e.dma_start(out=blko_bf[:], in_=blko_in), "ldg", 16)
    r.emit("gpsimd", lambda e: e.dma_start(out=sel64_bf[:], in_=sel64_in), "ldg", 16)
    r.emit("gpsimd", lambda e: TLOAD(e, wqkv_sb, wqkvT, 16), "ldg", 16)
    r.emit("gpsimd", lambda e: TLOAD(e, wzba_sb, wzbaT, 16), "ldg", 16)
    for dst, src in ((eye_f, eye_in), (mubn, mubn_in), (mube, mube_in),
                     (cw_sb, cw_in), (tiny_sb, tiny_in), (nw_sb, nw_in),
                     (gs_t, gs_in), (us_t, us_in), (ds_row, ds_in),
                     (cid_sb, cid_in)):
        r.emit("sync", lambda e, dst=dst, src=src: e.dma_start(out=dst[:], in_=src),
               "ld", 16)
    r.emit("vector", lambda e: e.memset(ones1_f[:], 1.0), "ve")
    r.emit("vector", lambda e: e.memset(cst[:, 0:1], 1e-5), "ve")
    r.emit("vector", lambda e: e.memset(cst[:, 1:2], 1e-6), "ve")
    r.emit("vector", lambda e: e.memset(cst[:, 2:3], 1.0), "ve")
    r.emit("vector", lambda e: e.memset(eps_col[0:64, :], 8e-12), "ve")
    r.emit("vector", lambda e: e.memset(eps_col[64:128, :], 1e-12), "ve")
    r.emit("vector", lambda e: e.memset(onesb[:], 1.0), "ve")
    r.wait("vector", "ld")
    r.wait("vector", "ldg")
    r.wait("scalar", "ld")
    r.wait("scalar", "ldg")
    r.wait("tensor", "ld")
    r.wait("tensor", "ldg")
    for j in range(12):
        r.emit("vector", lambda e, j=j: e.tensor_scalar(
            diag_bf[:, j * 128:(j + 1) * 128], eye_f[:], cw_sb[:, j:j + 1], None,
            AL.mult), "ve")
    r.emit("scalar", lambda e: e.activation(
        tiny_sb[0:1, 2:4], tiny_sb[0:1, 2:4], AF.Exp), "se")
    r.wait("tensor", "se")
    r.wait("tensor", "ve")
    r.emit("tensor", lambda e: e.matmul(
        pC0[:, 0:4], ones1_f[:], tiny_sb[0:1, 0:4], start=True, stop=True), "mm")
    r.wait("vector", "mm")
    r.emit("vector", lambda e: e.tensor_copy(dtb_bc[:], pC0[:, 0:2]), "ve")
    r.emit("vector", lambda e: e.tensor_scalar(
        negA_bc[:], pC0[:, 2:4], -1.0, None, AL.mult), "ve")
    r.wait("tensor", "ve")
    r.emit("tensor", lambda e: e.matmul(
        pC0[:, 128:256], ones1_f[:], nw_sb[0:1, :], start=True, stop=True), "mm")
    r.wait("vector", "mm")
    r.emit("vector", lambda e: e.tensor_copy(nw2_bc[:, 0:128], pC0[:, 128:256]), "ve")
    r.emit("vector", lambda e: e.tensor_copy(nw2_bc[:, 128:256], pC0[:, 128:256]), "ve")
    for j in range(4):
        r.wait("tensor", "ve")
        r.emit("tensor", lambda e, j=j: e.matmul(
            pC0[:, 0:512], ones1_f[:], ds_row[0:1, j * 512:(j + 1) * 512],
            start=True, stop=True), "mm")
        r.wait("vector", "mm")
        r.emit("vector", lambda e, j=j: e.tensor_copy(
            ds_bc[:, j * 512:(j + 1) * 512], pC0[:, 0:512]), "ve")

    r.mark = {}
    r.mark['p0'] = r.seq

    for i in range(TPC // 128):
        bi = i % 2
        if i >= 2:
            r.wait("sync", "ve")
        r.emit("sync", lambda e, i=i, bi=bi: e.dma_start(
            out=x0_sb[bi][:], in_=x_sl[128 * i:128 * (i + 1), :]), "lnld", 16)
        r.wait("vector", "lnld")
        if i > 0:
            r.wait("vector", "lnst")
        r.emit("vector", lambda e, bi=bi: e.tensor_reduce(
            st0[:, 0:1], x0_sb[bi][:], AX.X, AL.add, negate=True), "ve")
        r.wait("vector", "ve")
        r.emit("vector", lambda e: e.tensor_scalar(
            st0[:, 1:2], st0[:, 0:1], 1.0 / HID, None, AL.mult), "ve")
        r.wait("scalar", "ve")
        r.emit("scalar", lambda e, bi=bi: e.activation(
            junk0[:], x0_sb[bi][:], AF.Square, bias=st0[:, 1:2],
            accum_out=st0[:, 2:3]), "se")
        r.emit("scalar", lambda e: e.activation(
            st0[:, 3:4], st0[:, 2:3], AF.Sqrt, bias=cst[:, 0:1], scale=1.0 / HID), "se")
        r.wait("vector", "se")
        r.emit("vector", lambda e: e.reciprocal(st0[:, 4:5], st0[:, 3:4]), "ve")
        r.wait("vector", "ve")
        r.emit("vector", lambda e, bi=bi: e.tensor_scalar(
            lnbf_sb[:], x0_sb[bi][:], st0[:, 1:2], st0[:, 4:5], AL.add, AL.mult), "ve")
        r.wait("sync", "ve")
        r.emit("sync", lambda e, i=i: e.dma_start(
            out=ln_sl_d[128 * i:128 * (i + 1), :], in_=lnbf_sb[:]), "lnst", 16)

    r.mark['p0end'] = r.seq
    if hw_debug:
        r.wait("sync", "lnst")
        r.emit("sync", lambda e: e.dma_start(out=hwdump["lnsl"], in_=ln_sl_d), "dbgs", 16)
    r.wait("gpsimd", "lnst")
    r.emit("gpsimd", lambda e: e.collective_compute(
        "AllGather", AL.bypass, replica_groups=[list(range(NCORES))],
        ins=[ln_sl_d.opt()], outs=[ln_full_d.opt()]), "cc")

    if hw_debug:
        r.wait("sync", "cc", 1)
        r.emit("sync", lambda e: e.dma_start(out=hwdump["lnfull"], in_=ln_full_d), "dbgs", 16)
        r.wait("sync", "ve")
        r.emit("sync", lambda e: e.dma_start(out=hwdump["negA"], in_=negA_bc[:]), "dbgs", 16)
        r.emit("sync", lambda e: e.dma_start(out=hwdump["dtbb"], in_=dtb_bc[:]), "dbgs", 16)
        r.emit("sync", lambda e: e.dma_start(out=hwdump["nw2"], in_=nw2_bc[:]), "dbgs", 16)
    r.mark['p12'] = r.seq
    p12 = ExitStack()
    S1 = lambda n, shp, dt: p12.enter_context(nc.sbuf_tensor(n, shp, dt))
    P1 = lambda n, shp, dt=F32: p12.enter_context(nc.psum_tensor(n, shp, dt))
    lnT = [S1(f"lnT{i}", [128, 16 * TB], BF16) for i in range(2)]
    pre_qk = [S1(f"preqk{i}", [128, TB + 3], BF16) for i in range(2)]
    pre_v0 = [S1(f"prev0{i}", [128, TB + 3], BF16) for i in range(2)]
    pre_v1 = [S1(f"prev1{i}", [128, TB + 3], BF16) for i in range(2)]
    qkf = S1("qkf", [128, TB], F32)
    sq2 = S1("sq2", [128, TB], BF16)
    qkn = S1("qkn", [128, TB], BF16)
    kn0 = S1("kn0", [64, TB], BF16)
    v_bf = [S1(f"vbf{i}", [128, TB], BF16) for i in range(2)]
    nrm2 = S1("nrm2", [128, TB], F32)
    nwz_sb = S1("nwz_sb", [128, 256], F32)
    pc4 = S1("pc4", [128, 4], F32)
    gcol = S1("gcol", [128, 2], F32)
    tvm = S1("tvm", [128, 4], F32)
    bc_rows = S1("bc_rows", [1, 512], F32)
    e1_sb = S1("e1_sb", [128, 128], F32)
    df_sb = S1("df_sb", [128, 128], F32)
    mb_sb = S1("mb_sb", [128, 128], F32)
    nt1_sb = S1("nt1_sb", [128, 128], F32)
    nt_bf = S1("nt_bf", [128, 128], BF16)
    nn_bf = S1("nn_bf", [128, 128], BF16)
    ap_bf = S1("ap_bf", [128, 128], BF16)
    a1_sb = S1("a1_sb", [128, 128], F32)
    r_sb = S1("r_sb", [128, 192], BF16)
    y_bf = S1("y_bf", [128, 192], BF16)
    kt_bf = S1("kt_bf", [128, 64], BF16)
    kp_bf = S1("kp_bf", [128, 64], BF16)
    qg_sb = S1("qg_sb", [64, 128], F32)
    qpt_bf = S1("qpt_bf", [64, 128], BF16)
    pt_bf = S1("pt_bf", [64, 64], BF16)
    grow_sb = S1("grow_sb", [1, 128], F32)
    lamr = S1("lamr", [128, 1], F32)
    cl_sb = S1("cl_sb", [128, 1], F32)
    gam_col = S1("gam_col", [128, 1], F32)
    sc1 = S1("sc1", [128, 4], F32)
    osq = S1("osq", [128, 128], F32)
    o_f32 = S1("o_f32", [128, 128], F32)
    s_bf = S1("s_bf", [64, 2 * 128], BF16)
    gated_sb = [S1(f"gated{i}", [128, 256], BF16) for i in range(2)]

    pA = P1("pA", [128, 512])
    pBv = P1("pBv", [128, 512])
    pCv = P1("pCv", [128, 512])
    pY = P1("pY", [128, 192])
    pSq = P1("pSq", [128, 256])
    pTt = P1("pTt", [128, 384], BF16)
    pD = P1("pD", [128, 512])
    pO = pC0

    def tap(nm, src_ap, cond):
        if debug_chunk is None or not cond:
            return
        r.wait("sync", "ve")
        r.wait("sync", "se")
        r.wait("sync", "mm")
        r.emit("sync", lambda e, nm=nm, src_ap=src_ap: e.dma_start(
            out=dbg[nm], in_=src_ap), "dbgs", 16)

    def mm_group(psum_ap, lhsT_fn, rhs_fn, n):
        for t in range(n):
            r.emit("tensor", lambda e, t=t: e.matmul(
                psum_ap, lhsT_fn(t), rhs_fn(t), start=(t == 0), stop=(t == n - 1)),
                "mm" if t == n - 1 else None)

    for b in range(B):
        r.wait("vector", "mm")
        r.emit("vector", lambda e: e.memset(s_bf[:], 0.0), "ve")
        for tb in range(NTB):
            gtb = b * NTB + tb
            bi = gtb % 2
            pb = (gtb + 1) % 2
            tok0 = b * T + tb * TB
            r.wait("sync", "cc", 1)
            if gtb >= 2:
                r.wait("sync", "mm")
            for dt in range(16):
                r.emit("sync", lambda e, dt=dt, tok0=tok0, bi=bi: e.dma_start(
                    out=lnT[bi][:, dt * TB:(dt + 1) * TB],
                    in_=ln_full_d[tok0:tok0 + TB, dt * 128:(dt + 1) * 128],
                    transpose=True), "lnT", 16)
            r.wait("tensor", "lnT")
            r.wait("tensor", "ve")
            r.wait("tensor", "se")
            mm_group(pA[:, 0:512],
                     lambda t: wqkv_sb[:, 384 * t:384 * t + 128],
                     lambda t, bi=bi: lnT[bi][:, t * TB:(t + 1) * TB], 16)
            mm_group(pBv[:, 0:512],
                     lambda t: wqkv_sb[:, 384 * t + 128:384 * t + 256],
                     lambda t, bi=bi: lnT[bi][:, t * TB:(t + 1) * TB], 16)
            mm_group(pCv[:, 0:512],
                     lambda t: wqkv_sb[:, 384 * t + 256:384 * t + 384],
                     lambda t, bi=bi: lnT[bi][:, t * TB:(t + 1) * TB], 16)
            r.wait("vector", "mm")
            r.wait("scalar", "mm")
            for (dst, src, eng, sm) in ((pre_qk, pA, "vector", "ve"),
                                        (pre_v0, pBv, "scalar", "se"),
                                        (pre_v1, pCv, "scalar", "se")):
                if tb == 0:
                    r.emit("vector", lambda e, dst=dst, bi=bi: e.memset(dst[bi][:, 0:3], 0.0), "ve")
                elif True:
                    r.emit(eng, lambda e, dst=dst, bi=bi, pb=pb: COPY(e,
                        dst[bi][:, 0:3], dst[pb][:, TB:TB + 3]), sm)
                r.emit(eng, lambda e, dst=dst, src=src, bi=bi: COPY(e,
                    dst[bi][:, 3:TB + 3], src[:, 0:512]), sm)
            r.wait("tensor", "ve")
            r.wait("tensor", "se")
            for ct, (pre, psum) in enumerate(((pre_qk, pA), (pre_v0, pBv),
                                              (pre_v1, pCv))):
                for k in range(KS):
                    r.emit("tensor", lambda e, ct=ct, k=k, pre=pre, bi=bi, psum=psum:
                           e.matmul(psum[:, 0:512],
                                    diag_bf[:, (ct * 4 + k) * 128:(ct * 4 + k + 1) * 128],
                                    pre[bi][:, k:k + TB], start=(k == 0), stop=(k == 3)),
                           "mm" if k == 3 else None)
            r.wait("scalar", "mm")
            r.emit("scalar", lambda e: e.activation(qkf[:], pA[:, 0:512], AF.Silu), "se")
            r.emit("scalar", lambda e: e.activation(v_bf[0][:], pBv[:, 0:512], AF.Silu), "se")
            r.emit("scalar", lambda e: e.activation(v_bf[1][:], pCv[:, 0:512], AF.Silu), "se")
            r.wait("vector", "se")
            r.emit("vector", lambda e: e.tensor_mul(sq2[:], qkf[:], qkf[:]), "ve")
            r.wait("tensor", "ve")
            mm_group(pO[:, 0:512], lambda t: blko_bf[:], lambda t: sq2[:], 1)
            r.wait("scalar", "mm")
            r.emit("scalar", lambda e: e.activation(
                nrm2[0:64, :], pO[0:64, 0:512], AF.Sqrt, scale=64.0), "se")
            r.emit("scalar", lambda e: e.activation(
                nrm2[64:128, :], pO[64:128, 0:512], AF.Sqrt, scale=1.0), "se")
            r.wait("vector", "se")
            r.emit("vector", lambda e: e.tensor_scalar(
                nrm2[:], nrm2[:], eps_col[:], None, AL.max), "ve")
            r.wait("vector", "ve")
            r.emit("vector", lambda e: e.reciprocal(nrm2[:], nrm2[:]), "ve")
            r.wait("vector", "ve")
            r.emit("vector", lambda e: e.tensor_mul(qkn[:], qkf[:], nrm2[:]), "ve")
            r.wait("tensor", "ve")
            mm_group(pO[0:64, 0:512], lambda t: sel64_bf[:], lambda t: qkn[:], 1)
            r.wait("scalar", "mm")
            r.emit("scalar", lambda e: e.copy(kn0[:], pO[0:64, 0:512]), "se")
            tcond = debug_chunk is not None and gtb == debug_chunk[0]
            tap("qkn", qkn[:], tcond)
            tap("kn0", kn0[:], tcond)
            tap("v0", v_bf[0][:], tcond)
            tap("v1", v_bf[1][:], tcond)
            for cidx in range(4):
                cs = cidx * 128
                zp = (pBv, pCv, pBv, pCv)[cidx]
                tok0c = tok0 + cs
                gbi = (gtb * 4 + cidx) % 2
                ccond = debug_chunk is not None and (gtb, cidx) == tuple(debug_chunk)
                tap("sin", s_bf[:], ccond)
                r.wait("tensor", "se")
                r.wait("tensor", "ve")
                mm_group(zp[:, 0:260],
                         lambda t, bi=bi, cs=cs: lnT[bi][:, t * TB + cs:t * TB + cs + 128],
                         lambda t: wzba_sb[:, 260 * t:260 * t + 260], 16)
                r.wait("scalar", "mm")
                r.emit("scalar", lambda e, zp=zp: e.activation(
                    nwz_sb[:], zp[:, 0:256], AF.Silu), "se")
                r.wait("vector", "se")
                r.emit("vector", lambda e: e.tensor_mul(nwz_sb[:], nwz_sb[:], nw2_bc[:]), "ve")
                r.emit("scalar", lambda e, zp=zp: e.activation(
                    pc4[:, 0:2], zp[:, 256:258], AF.Sigmoid), "se")
                for vh in range(2):
                    r.emit("vector", lambda e, zp=zp, vh=vh: e.tensor_scalar(
                        tvm[:, vh:vh + 1], zp[:, 258 + vh:259 + vh],
                        dtb_bc[:, vh:vh + 1], 0.0, AL.add, AL.max), "ve")
                    r.emit("vector", lambda e, zp=zp, vh=vh: e.tensor_scalar(
                        tvm[:, 2 + vh:3 + vh], zp[:, 258 + vh:259 + vh],
                        dtb_bc[:, vh:vh + 1], None, AL.add), "ve")
                r.wait("scalar", "ve")
                r.emit("scalar", lambda e: e.activation(tvm[:, 2:4], tvm[:, 2:4], AF.Abs), "se")
                r.emit("scalar", lambda e: e.activation(tvm[:, 2:4], tvm[:, 2:4], AF.Exp, scale=-1.0), "se")
                r.emit("scalar", lambda e: e.activation(tvm[:, 2:4], tvm[:, 2:4], AF.Ln, bias=cst[:, 2:3]), "se")
                r.wait("vector", "se")
                r.emit("vector", lambda e: e.tensor_add(tvm[:, 2:4], tvm[:, 2:4], tvm[:, 0:2]), "ve")
                r.wait("vector", "ve")
                r.emit("vector", lambda e: e.tensor_mul(gcol[:], tvm[:, 2:4], negA_bc[:]), "ve")
                tap("gcol", gcol[:], ccond)
                tap("tvm", tvm[:], ccond)
                r.wait("tensor", "ve")
                mm_group(pO[:, 500:502], lambda t: mube[:], lambda t: gcol[:], 1)
                r.wait("vector", "mm")
                r.emit("vector", lambda e: e.tensor_copy(pc4[:, 2:4], pO[:, 500:502]), "ve")
                r.wait("tensor", "ve")
                for kk in range(4):
                    r.emit("tensor", lambda e, kk=kk: e.transpose(
                        pO[0:1, 128 * kk:128 * (kk + 1)], pc4[:, kk:kk + 1], eye_f[:]),
                        "mm" if kk == 3 else None)
                r.wait("vector", "mm")
                r.emit("vector", lambda e: e.tensor_copy(bc_rows[:], pO[0:1, 0:512]), "ve")
                tap("pc4", pc4[:], ccond)
                tap("bcr", bc_rows[:], ccond)
                r.wait("tensor", "ve")
                r.emit("tensor", lambda e, cs=cs: e.transpose(
                    pTt[:, 128:192], kn0[:, cs:cs + 128], eye_bf[0:64, 0:64]))
                r.emit("tensor", lambda e, cs=cs: e.transpose(
                    pTt[:, 192:320], v_bf[0][:, cs:cs + 128], eye_bf[:]), "mm")
                r.wait("scalar", "mm")
                r.emit("scalar", lambda e: e.copy(kt_bf[:], pTt[:, 128:192]), "se")
                for vh in range(2):
                    r.wait("tensor", "ve")
                    r.wait("tensor", "se")
                    r.emit("tensor", lambda e, cs=cs: e.matmul(
                        pSq[:, 0:128], kn0[:, cs:cs + 128], kn0[:, cs:cs + 128],
                        start=True, stop=True))
                    r.emit("tensor", lambda e, cs=cs: e.matmul(
                        pSq[:, 128:256], kn0[:, cs:cs + 128], qkn[0:64, cs:cs + 128],
                        start=True, stop=True))
                    r.emit("tensor", lambda e, vh=vh: e.matmul(
                        pD[:, 0:128], ones1_f[:], bc_rows[0:1, 128 * (2 + vh):128 * (3 + vh)], start=True, stop=True))
                    r.emit("tensor", lambda e, vh=vh: e.matmul(
                        pD[:, 128:256], ones1_f[:], bc_rows[0:1, 128 * vh:128 * (vh + 1)], start=True, stop=True))
                    r.emit("tensor", lambda e, vh=vh: e.matmul(
                        pD[:, 384:385], ones1_f[:], bc_rows[0:1, 128 * (2 + vh) + 127:128 * (2 + vh) + 128],
                        start=True, stop=True), "mm")
                    r.wait("vector", "mm")
                    r.emit("vector", lambda e: e.tensor_copy(cl_sb[:], pD[:, 384:385]), "ve")
                    r.wait("scalar", "mm")
                    r.wait("scalar", "ve")
                    r.emit("scalar", lambda e, vh=vh: e.activation(
                        grow_sb[:], bc_rows[0:1, 128 * (2 + vh):128 * (3 + vh)], AF.Exp), "se")
                    r.emit("scalar", lambda e, vh=vh: e.activation(
                        gam_col[:], pc4[:, 2 + vh:3 + vh], AF.Exp), "se")
                    r.emit("scalar", lambda e, vh=vh: e.activation(
                        lamr[:], pc4[:, 2 + vh:3 + vh], AF.Exp, scale=-1.0,
                        bias=cl_sb[:]), "se")
                    r.wait("tensor", "se")
                    r.emit("tensor", lambda e: e.matmul(
                        pD[0:64, 256:384], ones1_f[0:1, 0:64], grow_sb[:],
                        start=True, stop=True), "mm")
                    r.wait("vector", "mm")
                    r.emit("vector", lambda e, vh=vh: e.tensor_scalar(
                        e1_sb[:], pD[:, 0:128], pc4[:, 2 + vh:3 + vh], 0.0,
                        AL.subtract, AL.min), "ve")
                    r.wait("scalar", "ve")
                    r.emit("scalar", lambda e: e.activation(df_sb[:], e1_sb[:], AF.Exp), "se")
                    r.emit("vector", lambda e: e.tensor_mul(mb_sb[:], pD[:, 128:256], mubn[:]), "ve")
                    r.wait("vector", "se")
                    r.emit("vector", lambda e: e.tensor_mul(nt1_sb[:], df_sb[:], pSq[:, 0:128]), "ve")
                    r.wait("vector", "ve")
                    r.emit("vector", lambda e: e.tensor_mul(nt_bf[:], nt1_sb[:], mb_sb[:]), "ve")
                    r.emit("vector", lambda e: e.tensor_mul(a1_sb[:], df_sb[:], pSq[:, 128:256]), "ve")
                    r.wait("vector", "ve")
                    r.emit("vector", lambda e: e.tensor_mul(ap_bf[:], a1_sb[:], mube[:]), "ve")
                    tap(f"df{vh}", df_sb[:], ccond)
                    tap(f"nt{vh}", nt_bf[:], ccond)
                    r.wait("tensor", "ve")
                    r.emit("tensor", lambda e: e.transpose(pTt[:, 0:128], nt_bf[:], eye_bf[:]), "mm")
                    r.wait("scalar", "mm")
                    r.emit("scalar", lambda e: e.copy(nn_bf[:], pTt[:, 0:128]), "se")
                    if vh == 1:
                        r.wait("tensor", "se")
                        r.wait("tensor", "ve")
                        r.emit("tensor", lambda e, cs=cs: e.transpose(
                            pTt[:, 192:320], v_bf[1][:, cs:cs + 128], eye_bf[:]), "mm")
                    r.wait("vector", "mm")
                    r.wait("vector", "se")
                    r.emit("vector", lambda e, vh=vh: e.tensor_scalar(
                        r_sb[:, 0:128], pTt[:, 192:320], pc4[:, vh:vh + 1], None, AL.mult), "ve")
                    r.emit("vector", lambda e, vh=vh: e.tensor_scalar(
                        r_sb[:, 128:192], kt_bf[:], pc4[:, vh:vh + 1], gam_col[:],
                        AL.mult, AL.mult), "ve")
                    r.emit("vector", lambda e: e.tensor_scalar(
                        kp_bf[:], kt_bf[:], lamr[:], None, AL.mult), "ve")
                    tap(f"nn{vh}", nn_bf[:], ccond)
                    tap(f"ap{vh}", ap_bf[:], ccond)
                    tap(f"r{vh}", r_sb[:], ccond)
                    tap(f"kp{vh}", kp_bf[:], ccond)
                    r.wait("tensor", "ve")
                    r.wait("tensor", "se")
                    r.emit("tensor", lambda e: e.matmul(pY[:], eye_bf[:], r_sb[:], start=True, stop=False))
                    r.emit("tensor", lambda e: e.matmul(pY[:], nt_bf[:], r_sb[:], start=False, stop=True), "mm")
                    r.wait("vector", "mm")
                    r.emit("vector", lambda e: e.tensor_copy(y_bf[:], pY[:]), "ve")
                    for lev in range(1, 7):
                        r.emit("tensor", lambda e: e.matmul(pSq[:, 0:128], nt_bf[:], nn_bf[:], start=True, stop=True))
                        r.emit("tensor", lambda e: e.matmul(pSq[:, 128:256], nn_bf[:], nt_bf[:], start=True, stop=True), "mm")
                        r.wait("scalar", "mm")
                        r.emit("scalar", lambda e: e.copy(nn_bf[:], pSq[:, 0:128]), "se")
                        r.emit("scalar", lambda e: e.copy(nt_bf[:], pSq[:, 128:256]), "se")
                        r.wait("tensor", "se")
                        r.wait("tensor", "ve")
                        r.emit("tensor", lambda e: e.matmul(pY[:], nt_bf[:], y_bf[:], start=True, stop=True), "mm")
                        r.wait("vector", "mm")
                        r.wait("vector", "ve")
                        r.emit("vector", lambda e: e.tensor_tensor(y_bf[:], pY[:], y_bf[:], AL.add), "ve")
                    tap(f"yb{vh}", y_bf[:], ccond)
                    r.wait("tensor", "se")
                    r.emit("tensor", lambda e: e.matmul(
                        pSq[0:64, 0:128], y_bf[:, 128:192], ap_bf[:], start=True, stop=True))
                    r.emit("tensor", lambda e: e.matmul(
                        pSq[0:64, 128:192], y_bf[:, 128:192], kp_bf[:], start=True, stop=True), "mm")
                    r.wait("vector", "mm")
                    r.emit("vector", lambda e, cs=cs: e.tensor_mul(
                        qg_sb[:], qkn[0:64, cs:cs + 128], pD[0:64, 256:384]), "ve")
                    r.wait("vector", "ve")
                    r.emit("vector", lambda e: e.tensor_sub(qpt_bf[:], qg_sb[:], pSq[0:64, 0:128]), "ve")
                    r.emit("scalar", lambda e: e.activation(
                        sc1[:, 3:4], cl_sb[:], AF.Exp), "se")
                    r.wait("vector", "se")
                    r.emit("vector", lambda e: e.scalar_tensor_tensor(
                        pt_bf[:], eye_f[0:64, 0:64], sc1[0:64, 3:4], pSq[0:64, 128:192],
                        AL.mult, AL.subtract), "ve")
                    tap(f"qpt{vh}", qpt_bf[:], ccond)
                    tap(f"pt{vh}", pt_bf[:], ccond)
                    r.wait("tensor", "ve")
                    r.emit("tensor", lambda e: e.matmul(
                        pO[:, 0:128], ap_bf[:], y_bf[:, 0:128], start=True, stop=False))
                    r.emit("tensor", lambda e, vh=vh: e.matmul(
                        pO[:, 0:128], qpt_bf[:], s_bf[:, 128 * vh:128 * (vh + 1)],
                        start=False, stop=True))
                    r.emit("tensor", lambda e: e.matmul(
                        pO[0:64, 128:256], kp_bf[:], y_bf[:, 0:128], start=True, stop=False))
                    r.emit("tensor", lambda e, vh=vh: e.matmul(
                        pO[0:64, 128:256], pt_bf[:], s_bf[:, 128 * vh:128 * (vh + 1)],
                        start=False, stop=True), "mm")
                    r.wait("vector", "mm")
                    r.emit("vector", lambda e, vh=vh: e.tensor_copy(
                        s_bf[:, 128 * vh:128 * (vh + 1)], pO[0:64, 128:256]), "ve")
                    r.emit("vector", lambda e: e.tensor_copy(o_f32[:], pO[:, 0:128]), "ve")
                    r.wait("scalar", "ve")
                    r.emit("scalar", lambda e: e.activation(
                        osq[:], o_f32[:], AF.Square, accum_out=sc1[:, 0:1]), "se")
                    r.emit("scalar", lambda e: e.activation(
                        sc1[:, 1:2], sc1[:, 0:1], AF.Sqrt, bias=cst[:, 1:2], scale=1.0 / DV), "se")
                    r.wait("vector", "se")
                    r.emit("vector", lambda e: e.reciprocal(sc1[:, 2:3], sc1[:, 1:2]), "ve")
                    r.wait("vector", "ve")
                    if gtb * 4 + cidx >= 2:
                        r.wait("vector", "gst")
                    r.emit("vector", lambda e, vh=vh, gbi=gbi: e.scalar_tensor_tensor(
                        gated_sb[gbi][:, 128 * vh:128 * (vh + 1)], o_f32[:],
                        sc1[:, 2:3], nwz_sb[:, 128 * vh:128 * (vh + 1)],
                        AL.mult, AL.mult), "ve")
                tap("sout", s_bf[:], ccond)
                r.wait("sync", "ve")
                r.emit("sync", lambda e, tok0c=tok0c, gbi=gbi: e.dma_start(
                    out=g_sl_d[tok0c:tok0c + 128, :], in_=gated_sb[gbi][:]), "gst", 16)

    r.mark['p12end'] = r.seq
    if hw_debug:
        r.wait("sync", "gst")
        r.emit("sync", lambda e: e.dma_start(out=hwdump["gsl"], in_=g_sl_d), "dbgs", 16)
    r.wait("gpsimd", "gst")
    r.emit("gpsimd", lambda e: e.collective_compute(
        "AllGather", AL.bypass, replica_groups=[list(range(NCORES))],
        ins=[g_sl_d.opt()], outs=[g_full_d.opt()]), "cc")

    def reshuffle(e, S):
        v = e.value_load(cid_sb[0:1, 0:1], min_val=0, max_val=NCORES - 1)
        off = v * TPC
        for k in range(NCORES):
            e.dma_start(out=g_mine_d[0:TPC, 256 * k:256 * (k + 1)],
                        in_=g_full_d[bass.ds(k * TOKS + off, TPC), 0:256]
                        ).then_inc(S["gmv"], 16)
    r.wait("gpsimd", "cc")
    r.raw("gpsimd", reshuffle)
    r.cnt["gmv"] += 16 * NCORES

    if hw_debug:
        r.wait("sync", "gmv")
        r.emit("sync", lambda e: e.dma_start(out=hwdump["gmine"], in_=g_mine_d), "dbgs", 16)
    r.mark['p3'] = r.seq
    p12.close()
    p0.close()
    p3 = ExitStack()
    S3 = lambda n, shp, dt: p3.enter_context(nc.sbuf_tensor(n, shp, dt))
    P3 = lambda n, shp, dt=F32: p3.enter_context(nc.psum_tensor(n, shp, dt))
    h1T = S3("h1T", [128, 16 * HALF], BF16)
    oac = S3("oac", [128, NTT3 * HID], F32)
    st3 = S3("st3", [128, 8], F32)
    gsl_sb = S3("gsl_sb", [128, 512], F32)

    pPBig = P3("pPBig", [128, 2048])
    pG = P3("pG", [128, 512])
    pU = P3("pU", [128, 512])
    pT3 = P3("pT3", [128, 512], BF16)

    for eng in ENGINES:
        r.wait(eng, "cc")
        r.wait(eng, "mm")
        r.wait(eng, "ve")
        r.wait(eng, "se")
    r.wait("sync", "gmv")

    for hf in range(NHALF):
        htok0 = hf * HALF
        pA3 = ExitStack()
        SA = lambda n, shp, dt: pA3.enter_context(nc.sbuf_tensor(n, shp, dt))
        wo_sb = [SA(f"wo{i}_{hf}", [128, 4 * HID], BF16) for i in range(2)]
        gl_sb = [SA(f"gl{i}_{hf}", [128, 4 * HALF], BF16) for i in range(2)]
        x3_sb = SA(f"x3_{hf}", [128, HID], F32)
        o1_sb = SA(f"o1_{hf}", [128, HID], F32)
        h1bf = SA(f"h1bf_{hf}", [128, HID], BF16)
        for cseg in range(4):
            wb = cseg % 2
            if hf + cseg > 0:
                r.wait("gpsimd", "mm")
                r.wait("sync", "mm")
                if cseg == 0:
                    for sname in ("ve", "se", "yst"):
                        r.wait("gpsimd", sname)
                        r.wait("sync", sname)
            r.emit("gpsimd", lambda e, cseg=cseg, wb=wb, wo_sb=wo_sb: TLOAD(
                e, wo_sb[wb], woutT_in[512 * cseg:512 * (cseg + 1), :], 4), "p3w", 16)
            for cb in range(4):
                gc = cseg * 4 + cb
                r.emit("sync", lambda e, gc=gc, wb=wb, cb=cb, htok0=htok0, gl_sb=gl_sb:
                       e.dma_start(
                           out=gl_sb[wb][:, cb * HALF:(cb + 1) * HALF],
                           in_=g_mine_d[htok0:htok0 + HALF, gc * 128:(gc + 1) * 128],
                           transpose=True), "p3g", 16)
            r.wait("tensor", "p3w")
            r.wait("tensor", "p3g")
            r.wait("tensor", "se")
            for tt in range(NTT3):
                r.wait("tensor", "ve")
                for ob in range(4):
                    for cb in range(4):
                        r.emit("tensor",
                               lambda e, tt=tt, ob=ob, cb=cb, wb=wb, wo_sb=wo_sb, gl_sb=gl_sb:
                               e.matmul(pPBig[:, ob * 512:(ob + 1) * 512],
                                        gl_sb[wb][:, cb * HALF + tt * 128:cb * HALF + (tt + 1) * 128],
                                        wo_sb[wb][:, cb * HID + ob * 512:cb * HID + (ob + 1) * 512],
                                        start=(cb == 0), stop=(cb == 3)),
                               "mm" if (ob == 3 and cb == 3) else None)
                r.wait("vector", "mm")
                for ob in range(4):
                    if cseg == 0:
                        r.emit("vector", lambda e, tt=tt, ob=ob: e.tensor_copy(
                            oac[:, tt * HID + ob * 512:tt * HID + (ob + 1) * 512],
                            pPBig[:, ob * 512:(ob + 1) * 512]), "ve")
                    else:
                        r.emit("vector", lambda e, tt=tt, ob=ob: e.tensor_add(
                            oac[:, tt * HID + ob * 512:tt * HID + (ob + 1) * 512],
                            oac[:, tt * HID + ob * 512:tt * HID + (ob + 1) * 512],
                            pPBig[:, ob * 512:(ob + 1) * 512]), "ve")
        for tt in range(NTT3):
            gtt = hf * NTT3 + tt
            r.wait("sync", "ve")
            r.emit("sync", lambda e, gtt=gtt, x3_sb=x3_sb: e.dma_start(
                out=x3_sb[:], in_=x_sl[128 * gtt:128 * (gtt + 1), :]), "p3x", 16)
            r.wait("vector", "ve")
            r.emit("vector", lambda e, tt=tt: e.tensor_reduce(
                st3[:, 0:1], oac[:, tt * HID:(tt + 1) * HID], AX.X, AL.add,
                negate=True), "ve")
            r.wait("vector", "ve")
            r.emit("vector", lambda e: e.tensor_scalar(
                st3[:, 1:2], st3[:, 0:1], 1.0 / HID, None, AL.mult), "ve")
            r.wait("scalar", "ve")
            r.emit("scalar", lambda e, tt=tt: e.activation(
                pPBig[:], oac[:, tt * HID:(tt + 1) * HID], AF.Square, bias=st3[:, 1:2],
                accum_out=st3[:, 2:3]), "se")
            r.emit("scalar", lambda e: e.activation(
                st3[:, 3:4], st3[:, 2:3], AF.Sqrt, bias=cst[:, 0:1], scale=1.0 / HID), "se")
            r.wait("vector", "se")
            r.emit("vector", lambda e: e.reciprocal(st3[:, 4:5], st3[:, 3:4]), "ve")
            r.wait("vector", "ve")
            if gtt > 0:
                r.wait("vector", "o1st")
            r.wait("vector", "p3x")
            r.emit("vector", lambda e, tt=tt, o1_sb=o1_sb: e.tensor_scalar(
                o1_sb[:], oac[:, tt * HID:(tt + 1) * HID], st3[:, 1:2], st3[:, 4:5],
                AL.add, AL.mult), "ve")
            r.wait("vector", "ve")
            r.emit("vector", lambda e, o1_sb=o1_sb, x3_sb=x3_sb: e.tensor_add(
                o1_sb[:], o1_sb[:], x3_sb[:]), "ve")
            r.wait("sync", "ve")
            r.emit("sync", lambda e, gtt=gtt, o1_sb=o1_sb: e.dma_start(
                out=out1_d[128 * gtt:128 * (gtt + 1), :], in_=o1_sb[:]), "o1st", 16)
            r.wait("vector", "ve")
            r.emit("vector", lambda e, o1_sb=o1_sb: e.tensor_reduce(
                st3[:, 0:1], o1_sb[:], AX.X, AL.add, negate=True), "ve")
            r.wait("vector", "ve")
            r.emit("vector", lambda e: e.tensor_scalar(
                st3[:, 1:2], st3[:, 0:1], 1.0 / HID, None, AL.mult), "ve")
            r.wait("scalar", "ve")
            r.emit("scalar", lambda e, o1_sb=o1_sb: e.activation(
                pPBig[:], o1_sb[:], AF.Square, bias=st3[:, 1:2], accum_out=st3[:, 2:3]), "se")
            r.emit("scalar", lambda e: e.activation(
                st3[:, 3:4], st3[:, 2:3], AF.Sqrt, bias=cst[:, 0:1], scale=1.0 / HID), "se")
            r.wait("vector", "se")
            r.emit("vector", lambda e: e.reciprocal(st3[:, 4:5], st3[:, 3:4]), "ve")
            r.wait("vector", "ve")
            r.emit("vector", lambda e, o1_sb=o1_sb, h1bf=h1bf: e.tensor_scalar(
                h1bf[:], o1_sb[:], st3[:, 1:2], st3[:, 4:5], AL.add, AL.mult), "ve")
            r.wait("tensor", "ve")
            for dt in range(16):
                r.emit("tensor", lambda e, dt=dt, h1bf=h1bf: e.transpose(
                    pT3[:, (dt % 4) * 128:(dt % 4 + 1) * 128],
                    h1bf[:, dt * 128:(dt + 1) * 128], eye_bf[:]),
                    "mm" if dt % 4 == 3 else None)
                if dt % 4 == 3:
                    r.wait("scalar", "mm")
                    for q in range(4):
                        dq = dt - 3 + q
                        r.emit("scalar", lambda e, dq=dq, tt=tt: e.copy(
                            h1T[:, dq * HALF + tt * 128:dq * HALF + (tt + 1) * 128],
                            pT3[:, (dq % 4) * 128:(dq % 4 + 1) * 128]), "se")
                    r.wait("tensor", "se")
        pA3.close()
        for sname in ("mm", "ve", "se", "o1st"):
            r.wait("gpsimd", sname)
        pB3 = ExitStack()
        SBp = lambda n, shp, dt: pB3.enter_context(nc.sbuf_tensor(n, shp, dt))
        hseg = SBp(f"hseg_{hf}", [128, SEG * HALF], BF16)
        gw_sb = [SBp(f"gw{i}_{hf}", [128, 16 * 128], BF16) for i in range(2)]
        uw_sb = [SBp(f"uw{i}_{hf}", [128, 16 * 128], BF16) for i in range(2)]
        dw_sb = [SBp(f"dw{i}_{hf}", [128, SEG * 512], BF16) for i in range(2)]
        for iseg in range(NSEG):
            for il in range(SEG):
                it = iseg * SEG + il
                ib = it % 2
                if hf + it > 0:
                    r.wait("gpsimd", "mm")
                r.emit("gpsimd", lambda e, it=it, ib=ib, gw_sb=gw_sb: TLOAD(
                    e, gw_sb[ib], gwT_in[:, 128 * it:128 * (it + 1)], 16), "p3w", 16)
                r.emit("gpsimd", lambda e, it=it, ib=ib, uw_sb=uw_sb: TLOAD(
                    e, uw_sb[ib], uwT_in[:, 128 * it:128 * (it + 1)], 16), "p3w", 16)
                r.wait("tensor", "p3w")
                for tg in range(2):
                    r.wait("tensor", "ve")
                    for dt in range(16):
                        r.emit("tensor", lambda e, dt=dt, tg=tg, ib=ib, gw_sb=gw_sb: e.matmul(
                            pG[:], gw_sb[ib][:, dt * 128:(dt + 1) * 128],
                            h1T[:, dt * HALF + tg * 512:dt * HALF + (tg + 1) * 512],
                            start=(dt == 0), stop=(dt == 15)))
                    for dt in range(16):
                        r.emit("tensor", lambda e, dt=dt, tg=tg, ib=ib, uw_sb=uw_sb: e.matmul(
                            pU[:], uw_sb[ib][:, dt * 128:(dt + 1) * 128],
                            h1T[:, dt * HALF + tg * 512:dt * HALF + (tg + 1) * 512],
                            start=(dt == 0), stop=(dt == 15)),
                            "mm" if dt == 15 else None)
                    r.wait("scalar", "mm")
                    r.emit("scalar", lambda e, it=it: e.activation(
                        gsl_sb[:], pG[:], AF.Silu, scale=gs_t[:, it:it + 1]), "se")
                    r.wait("vector", "se")
                    r.emit("vector", lambda e, it=it, il=il, tg=tg, hseg=hseg:
                           e.scalar_tensor_tensor(
                               hseg[:, il * HALF + tg * 512:il * HALF + (tg + 1) * 512],
                               pU[:], us_t[:, it:it + 1], gsl_sb[:], AL.mult, AL.mult), "ve")
            r.wait("tensor", "ve")
            for ob in range(4):
                db = ob % 2
                if hf + iseg + ob > 0:
                    r.wait("gpsimd", "mm")
                r.emit("gpsimd", lambda e, iseg=iseg, ob=ob, db=db, dw_sb=dw_sb: TLOAD(
                    e, dw_sb[db], dwT_in[128 * SEG * iseg:128 * SEG * (iseg + 1),
                                         512 * ob:512 * (ob + 1)], SEG), "p3w", 16)
                r.wait("tensor", "p3w")
                for tt in range(NTT3):
                    obD = 2 + (tt % 2)
                    r.wait("tensor", "ve")
                    for il in range(SEG):
                        r.emit("tensor", lambda e, il=il, tt=tt, db=db, obD=obD,
                               hseg=hseg, dw_sb=dw_sb: e.matmul(
                                   pPBig[:, obD * 512:(obD + 1) * 512],
                                   hseg[:, il * HALF + tt * 128:il * HALF + (tt + 1) * 128],
                                   dw_sb[db][:, il * 512:(il + 1) * 512],
                                   start=(il == 0), stop=(il == SEG - 1)),
                               "mm" if il == SEG - 1 else None)
                    r.wait("vector", "mm")
                    if iseg == 0:
                        r.emit("vector", lambda e, tt=tt, ob=ob, obD=obD: e.tensor_copy(
                            oac[:, tt * HID + ob * 512:tt * HID + (ob + 1) * 512],
                            pPBig[:, obD * 512:(obD + 1) * 512]), "ve")
                    else:
                        r.emit("vector", lambda e, tt=tt, ob=ob, obD=obD: e.tensor_add(
                            oac[:, tt * HID + ob * 512:tt * HID + (ob + 1) * 512],
                            oac[:, tt * HID + ob * 512:tt * HID + (ob + 1) * 512],
                            pPBig[:, obD * 512:(obD + 1) * 512]), "ve")
        pB3.close()
        for sname in ("mm", "ve", "se"):
            r.wait("sync", sname)
        pC3 = ExitStack()
        SCp = lambda n, shp, dt: pC3.enter_context(nc.sbuf_tensor(n, shp, dt))
        o1c = SCp(f"o1c_{hf}", [128, HID], F32)
        y_sb = SCp(f"y_{hf}", [128, HID], F32)
        r.wait("sync", "o1st")
        for tt in range(NTT3):
            gtt = hf * NTT3 + tt
            r.wait("sync", "ve")
            r.emit("sync", lambda e, gtt=gtt, o1c=o1c: e.dma_start(
                out=o1c[:], in_=out1_d[128 * gtt:128 * (gtt + 1), :]), "p3x", 16)
            r.emit("vector", lambda e, tt=tt: e.tensor_mul(
                oac[:, tt * HID:(tt + 1) * HID], oac[:, tt * HID:(tt + 1) * HID],
                ds_bc[:]), "ve")
            r.wait("vector", "ve")
            r.emit("vector", lambda e, tt=tt: e.tensor_reduce(
                st3[:, 0:1], oac[:, tt * HID:(tt + 1) * HID], AX.X, AL.add,
                negate=True), "ve")
            r.wait("vector", "ve")
            r.emit("vector", lambda e: e.tensor_scalar(
                st3[:, 1:2], st3[:, 0:1], 1.0 / HID, None, AL.mult), "ve")
            r.wait("scalar", "ve")
            r.emit("scalar", lambda e, tt=tt: e.activation(
                pPBig[:], oac[:, tt * HID:(tt + 1) * HID], AF.Square, bias=st3[:, 1:2],
                accum_out=st3[:, 2:3]), "se")
            r.emit("scalar", lambda e: e.activation(
                st3[:, 3:4], st3[:, 2:3], AF.Sqrt, bias=cst[:, 0:1], scale=1.0 / HID), "se")
            r.wait("vector", "se")
            r.emit("vector", lambda e: e.reciprocal(st3[:, 4:5], st3[:, 3:4]), "ve")
            r.wait("vector", "ve")
            if gtt > 0:
                r.wait("vector", "yst")
            r.wait("vector", "p3x")
            r.emit("vector", lambda e, tt=tt, y_sb=y_sb: e.tensor_scalar(
                y_sb[:], oac[:, tt * HID:(tt + 1) * HID], st3[:, 1:2], st3[:, 4:5],
                AL.add, AL.mult), "ve")
            r.wait("vector", "ve")
            r.emit("vector", lambda e, y_sb=y_sb, o1c=o1c: e.tensor_add(
                y_sb[:], y_sb[:], o1c[:]), "ve")
            r.wait("sync", "ve")
            r.emit("sync", lambda e, gtt=gtt, y_sb=y_sb: e.dma_start(
                out=y_out[128 * gtt:128 * (gtt + 1), :], in_=y_sb[:]), "yst", 16)
        pC3.close()

    if hw_debug:
        r.wait("sync", "o1st")
        r.emit("sync", lambda e: e.dma_start(out=hwdump["o1"], in_=out1_d), "dbgs", 16)
        r.wait("sync", "dbgs")
    if debug_chunk is not None:
        r.wait("sync", "dbgs")
        for nm in dbg:
            r.emit("sync", lambda e, nm=nm: e.dma_start(out=dbg_out[nm], in_=dbg[nm]),
                   "dbgs", 16)
        r.wait("sync", "dbgs")
    r.wait("sync", "yst")

    S = {}
    sems_ctx = ExitStack()
    for name in r.cnt:
        S[name] = sems_ctx.enter_context(nc.semaphore(name))
    with nc.Block() as block:
        @block.sync
        def _(e):
            for sq, fn in r.streams["sync"]:
                if trunc is None or sq < trunc:
                    fn(e, S)

        @block.tensor
        def _(e):
            for sq, fn in r.streams["tensor"]:
                if trunc is None or sq < trunc:
                    fn(e, S)

        @block.scalar
        def _(e):
            for sq, fn in r.streams["scalar"]:
                if trunc is None or sq < trunc:
                    fn(e, S)

        @block.vector
        def _(e):
            for sq, fn in r.streams["vector"]:
                if trunc is None or sq < trunc:
                    fn(e, S)

        @block.gpsimd
        def _(e):
            for sq, fn in r.streams["gpsimd"]:
                if trunc is None or sq < trunc:
                    fn(e, S)
    mybir.codegen_inst_isa_subclasses(nc)
    nc._rec_marks = dict(r.mark)
    return nc


def make_in_maps(inputs):
    x = np.asarray(inputs["L0_attn_x"], np.float32).reshape(TOKS, HID)
    Win = np.asarray(inputs["L0_attn_in_proj_w"], np.float32)
    convw = np.asarray(inputs["L0_attn_conv_w"], np.float32)
    dtb = np.asarray(inputs["L0_attn_dt_bias"], np.float32)
    Alog = np.asarray(inputs["L0_attn_A_log"], np.float32)
    nw = np.asarray(inputs["L0_attn_norm_weight"], np.float32)
    Wout = np.asarray(inputs["L0_attn_out_proj_w"], np.float32)
    gw = np.asarray(inputs["L0_ffn_gate_w"], np.float32)
    uw = np.asarray(inputs["L0_ffn_up_w"], np.float32)
    dw = np.asarray(inputs["L0_ffn_down_w"], np.float32)
    gs = np.asarray(inputs["L0_ffn_gate_s"], np.float32)
    us = np.asarray(inputs["L0_ffn_up_s"], np.float32)
    ds = np.asarray(inputs["L0_ffn_down_s"], np.float32)

    idx = np.arange(128)
    eye = np.eye(128, dtype=np.float32)
    mubn = np.where(idx[:, None] < idx[None, :], -1.0, 0.0).astype(np.float32)
    mube = np.where(idx[:, None] <= idx[None, :], 1.0, 0.0).astype(np.float32)
    blko = np.zeros((128, 128), np.float32)
    blko[:64, :64] = 1.0
    blko[64:, 64:] = 1.0
    sel64 = np.ascontiguousarray(np.eye(128, dtype=np.float32)[:, 64:])
    woutT = np.ascontiguousarray(Wout.T)
    gwT = np.ascontiguousarray(gw.T)
    uwT = np.ascontiguousarray(uw.T)
    dwT = np.ascontiguousarray(dw.T)
    gs_p = np.ascontiguousarray(gs.reshape(NIT, 128).T)
    us_p = np.ascontiguousarray(us.reshape(NIT, 128).T)
    ds_r = np.ascontiguousarray(ds.reshape(1, HID))

    in_maps = []
    for c in range(NCORES):
        Wq = Win[64 * c:64 * c + 64]
        Wk = Win[KD + 64 * c:KD + 64 * c + 64]
        Wv = Win[2 * KD + 256 * c:2 * KD + 256 * c + 256]
        Wz = Win[CD + 256 * c:CD + 256 * c + 256]
        Wb = Win[CD + VD + 2 * c:CD + VD + 2 * c + 2]
        Wa = Win[CD + VD + NVH + 2 * c:CD + VD + NVH + 2 * c + 2]
        cw = np.concatenate([convw[64 * c:64 * c + 64],
                             convw[KD + 64 * c:KD + 64 * c + 64],
                             convw[2 * KD + 256 * c:2 * KD + 256 * c + 256]], 0)
        in_maps.append({
            "x_sl": np.ascontiguousarray(x[c * TPC:(c + 1) * TPC]),
            "wqkvT": np.ascontiguousarray(np.concatenate([Wq, Wk, Wv], 0).T),
            "wzbaT": np.ascontiguousarray(np.concatenate([Wz, Wb, Wa], 0).T),
            "cw_in": np.ascontiguousarray(
                cw.reshape(3, 128, 4).transpose(1, 0, 2).reshape(128, 12)),
            "tiny_in": np.ascontiguousarray(np.concatenate(
                [dtb[2 * c:2 * c + 2], Alog[2 * c:2 * c + 2]]).reshape(1, 4)),
            "nw_in": np.ascontiguousarray(nw.reshape(1, DV)),
            "eye_in": eye, "mubn_in": mubn, "mube_in": mube,
            "blko_in": blko, "sel64_in": sel64,
            "cid_in": np.array([[c]], np.int32),
            "woutT_in": woutT, "gwT_in": gwT, "uwT_in": uwT, "dwT_in": dwT,
            "gs_in": gs_p, "us_in": us_p, "ds_in": ds_r,
        })
    return in_maps


_nc_cache = None


def kernel(**inputs):
    global _nc_cache
    in_maps = make_in_maps(inputs)
    if _nc_cache is None:
        _nc_cache = build_program()
    res = run_bass_kernel_spmd(_nc_cache, in_maps, core_ids=list(range(NCORES)))
    y = np.concatenate([np.asarray(r["y_out"], np.float32) for r in res.results], 0)
    return y.reshape(B, T, HID)


# version 06
# speedup vs baseline: 1.0825x; 1.0825x over previous
import numpy as np
from contextlib import ExitStack

import concourse.bass as bass
import concourse.mybir as mybir
from concourse.bass_utils import run_bass_kernel_spmd

F32 = mybir.dt.float32
BF16 = mybir.dt.bfloat16
I32 = mybir.dt.int32
AF = mybir.ActivationFunctionType
AL = mybir.AluOpType
AX = mybir.AxisListType

B, T, HID = 2, 8192, 2048
TOKS = B * T
NCORES = 8
TPC = TOKS // NCORES
NKH, NVH, DK, DV, KS = 8, 16, 64, 128, 4
KD, VD = NKH * DK, NVH * DV
CD = 2 * KD + VD
TB = 512
NTB = T // TB
SCALE = DK ** -0.5
INTER = 5632
NIT = INTER // 128
NSEG = 4
SEG = NIT // NSEG
HALF = 1024
NHALF = TPC // HALF
NTT3 = HALF // 128

ENGINES = ("sync", "tensor", "scalar", "vector", "gpsimd")


class Rec:
    def __init__(self):
        self.streams = {e: [] for e in ENGINES}
        self.cnt = {}
        self.seq = 0

    def _app(self, eng, fn):
        self.streams[eng].append((self.seq, fn))
        self.seq += 1

    def sem(self, name):
        self.cnt[name] = 0

    def emit(self, eng, fn, inc=None, inc_n=1):
        if inc is not None:
            self.cnt[inc] += inc_n
            self._app(eng, lambda e, S, fn=fn, inc=inc, n=inc_n: fn(e).then_inc(S[inc], n))
        else:
            self._app(eng, lambda e, S, fn=fn: fn(e))

    def raw(self, eng, fn):
        self._app(eng, fn)

    def wait(self, eng, sname, val=None):
        v = self.cnt[sname] if val is None else val
        if v > 0:
            self._app(eng, lambda e, S, sname=sname, v=v: e.wait_ge(S[sname], v))


def TLOAD(e, dst_sb, src_ap, a):
    return e.dma_start(out=dst_sb[:].rearrange("p (a c) -> p a c", a=a),
                       in_=src_ap.rearrange("(a p) c -> p a c", p=128))


def COPY(e, out, in_):
    if hasattr(e, "tensor_copy"):
        return e.tensor_copy(out, in_)
    return e.copy(out, in_)


def build_program(debug_chunk=None, trunc=None, hw_debug=False):
    nc = bass.Bass("TRN2", target_bir_lowering=False, num_devices=NCORES,
                   detect_race_conditions=False)
    r = Rec()
    for s in ("ld", "ldg", "lnld", "lnst", "cc", "lnT", "gst", "gmv", "p3w", "p3g",
              "p3x", "o1st", "yst", "mm", "ve", "se", "dbgs"):
        r.sem(s)

    di = lambda n, shp, dt=F32: nc.dram_tensor(n, shp, dt, kind="ExternalInput").ap()
    x_sl = di("x_sl", [TPC, HID])
    wqkvT = di("wqkvT", [HID, 384])
    wzbaT = di("wzbaT", [HID, 260])
    cw_in = di("cw_in", [128, 12])
    tiny_in = di("tiny_in", [1, 4])
    nw_in = di("nw_in", [1, DV])
    eye_in = di("eye_in", [128, 128])
    mubn_in = di("mubn_in", [128, 128])
    mube_in = di("mube_in", [128, 128])
    blko_in = di("blko_in", [128, 128])
    sel64_in = di("sel64_in", [128, 64])
    cid_in = di("cid_in", [1, 1], I32)
    woutT_in = di("woutT_in", [HID, HID])
    gwT_in = di("gwT_in", [HID, INTER])
    uwT_in = di("uwT_in", [HID, INTER])
    dwT_in = di("dwT_in", [INTER, HID])
    gs_in = di("gs_in", [128, NIT])
    us_in = di("us_in", [128, NIT])
    ds_in = di("ds_in", [1, HID])

    y_out = nc.dram_tensor("y_out", [TPC, HID], F32, kind="ExternalOutput").ap()

    ln_sl_d = nc.dram_tensor("ln_sl_d", [TPC, HID], BF16).ap()
    ln_full_d = nc.dram_tensor("ln_full_d", [TOKS, HID], BF16, addr_space="Shared").ap()
    g_sl_d = nc.dram_tensor("g_sl_d", [TOKS, 256], BF16).ap()
    g_full_d = nc.dram_tensor("g_full_d", [NCORES * TOKS, 256], BF16, addr_space="Shared").ap()
    g_mine_d = nc.dram_tensor("g_mine_d", [TPC, HID], BF16).ap()
    out1_d = nc.dram_tensor("out1_d", [TPC, HID], F32).ap()
    hwdump = {}
    if hw_debug:
        for nm, src_shape, dt in (("lnsl", [TPC, HID], BF16), ("gsl", [TOKS, 256], BF16),
                                  ("gmine", [TPC, HID], BF16), ("o1", [TPC, HID], F32),
                                  ("lnfull", [TOKS, HID], BF16), ("negA", [128, 2], F32),
                                  ("dtbb", [128, 2], F32), ("nw2", [128, 256], F32)):
            hwdump[nm] = nc.dram_tensor("hwd_" + nm, src_shape, dt, kind="ExternalOutput").ap()
    dbg = {}
    if debug_chunk is not None:
        for nm, shp, dt in (("qkn", [128, 512], BF16), ("kn0", [64, 512], BF16),
                            ("v0", [128, 512], BF16), ("v1", [128, 512], BF16),
                            ("pc4", [128, 4], F32), ("bcr", [1, 512], F32),
                            ("nt0", [128, 128], BF16), ("nt1", [128, 128], BF16),
                            ("nn0", [128, 128], BF16), ("nn1", [128, 128], BF16),
                            ("ap0", [128, 128], BF16), ("ap1", [128, 128], BF16),
                            ("r0", [128, 192], BF16), ("r1", [128, 192], BF16),
                            ("yb0", [128, 192], BF16), ("yb1", [128, 192], BF16),
                            ("qpt0", [64, 128], BF16), ("qpt1", [64, 128], BF16),
                            ("pt0", [64, 64], BF16), ("pt1", [64, 64], BF16),
                            ("kp0", [128, 64], BF16), ("kp1", [128, 64], BF16),
                            ("sin", [64, 256], BF16), ("sout", [64, 256], BF16),
                            ("df0", [128, 128], F32), ("df1", [128, 128], F32),
                            ("gcol", [128, 2], F32), ("tvm", [128, 4], F32)):
            dbg[nm] = nc.dram_tensor("dbg_" + nm, shp, dt).ap()
        dbg_out = {nm: nc.dram_tensor("dbgo_" + nm, list(dbg[nm].tensor.shape),
                                      dbg[nm].tensor.dtype, kind="ExternalOutput").ap()
                   for nm in dbg}

    ctx = ExitStack()
    SB = lambda n, shp, dt: ctx.enter_context(nc.sbuf_tensor(n, shp, dt))

    eye_f = SB("eye_f", [128, 128], F32)
    eye_bf = SB("eye_bf", [128, 128], BF16)
    mubn = SB("mubn", [128, 128], F32)
    mube = SB("mube", [128, 128], F32)
    ones1_f = SB("ones1_f", [1, 128], F32)
    onesb = SB("onesb", [64, 64], BF16)
    cw_sb = SB("cw_sb", [128, 12], F32)
    diag_bf = SB("diag_bf", [128, 12 * 128], BF16)
    tiny_sb = SB("tiny_sb", [1, 4], F32)
    nw_sb = SB("nw_sb", [1, DV], F32)
    dtb_bc = SB("dtb_bc", [128, 2], F32)
    negA_bc = SB("negA_bc", [128, 2], F32)
    nw2_bc = SB("nw2_bc", [128, 256], F32)
    gs_t = SB("gs_t", [128, NIT], F32)
    us_t = SB("us_t", [128, NIT], F32)
    ds_bc = SB("ds_bc", [128, HID], F32)
    cid_sb = SB("cid_sb", [1, 1], I32)
    cst = SB("cst", [128, 4], F32)
    wqkv_sb = SB("wqkv_sb", [128, 16 * 384], BF16)
    wzba_sb = SB("wzba_sb", [128, 16 * 260], BF16)
    blko_bf = SB("blko_bf", [128, 128], BF16)
    sel64_bf = SB("sel64_bf", [128, 64], BF16)
    eps_col = SB("eps_col", [128, 1], F32)

    pC0 = ctx.enter_context(nc.psum_tensor("pC0", [128, 512], F32))

    p0 = ExitStack()
    S0 = lambda n, shp, dt: p0.enter_context(nc.sbuf_tensor(n, shp, dt))
    ds_row = S0("ds_row", [1, HID], F32)
    x0_sb = [S0(f"x0_{i}", [128, HID], F32) for i in range(2)]
    junk0 = S0("junk0", [128, HID], F32)
    lnbf_sb = S0("lnbf", [128, HID], BF16)
    st0 = S0("st0", [128, 8], F32)

    r.emit("gpsimd", lambda e: e.dma_start(out=eye_bf[:], in_=eye_in), "ldg", 16)
    r.emit("gpsimd", lambda e: e.dma_start(out=blko_bf[:], in_=blko_in), "ldg", 16)
    r.emit("gpsimd", lambda e: e.dma_start(out=sel64_bf[:], in_=sel64_in), "ldg", 16)
    r.emit("gpsimd", lambda e: TLOAD(e, wqkv_sb, wqkvT, 16), "ldg", 16)
    r.emit("gpsimd", lambda e: TLOAD(e, wzba_sb, wzbaT, 16), "ldg", 16)
    for dst, src in ((eye_f, eye_in), (mubn, mubn_in), (mube, mube_in),
                     (cw_sb, cw_in), (tiny_sb, tiny_in), (nw_sb, nw_in),
                     (gs_t, gs_in), (us_t, us_in), (ds_row, ds_in),
                     (cid_sb, cid_in)):
        r.emit("sync", lambda e, dst=dst, src=src: e.dma_start(out=dst[:], in_=src),
               "ld", 16)
    r.emit("vector", lambda e: e.memset(ones1_f[:], 1.0), "ve")
    r.emit("vector", lambda e: e.memset(cst[:, 0:1], 1e-5), "ve")
    r.emit("vector", lambda e: e.memset(cst[:, 1:2], 1e-6), "ve")
    r.emit("vector", lambda e: e.memset(cst[:, 2:3], 1.0), "ve")
    r.emit("vector", lambda e: e.memset(eps_col[0:64, :], 8e-12), "ve")
    r.emit("vector", lambda e: e.memset(eps_col[64:128, :], 1e-12), "ve")
    r.emit("vector", lambda e: e.memset(onesb[:], 1.0), "ve")
    r.wait("vector", "ld")
    r.wait("vector", "ldg")
    r.wait("scalar", "ld")
    r.wait("scalar", "ldg")
    r.wait("tensor", "ld")
    r.wait("tensor", "ldg")
    for j in range(12):
        r.emit("vector", lambda e, j=j: e.tensor_scalar(
            diag_bf[:, j * 128:(j + 1) * 128], eye_f[:], cw_sb[:, j:j + 1], None,
            AL.mult), "ve")
    r.emit("scalar", lambda e: e.activation(
        tiny_sb[0:1, 2:4], tiny_sb[0:1, 2:4], AF.Exp), "se")
    r.wait("tensor", "se")
    r.wait("tensor", "ve")
    r.emit("tensor", lambda e: e.matmul(
        pC0[:, 0:4], ones1_f[:], tiny_sb[0:1, 0:4], start=True, stop=True), "mm")
    r.wait("vector", "mm")
    r.emit("vector", lambda e: e.tensor_copy(dtb_bc[:], pC0[:, 0:2]), "ve")
    r.emit("vector", lambda e: e.tensor_scalar(
        negA_bc[:], pC0[:, 2:4], -1.0, None, AL.mult), "ve")
    r.wait("tensor", "ve")
    r.emit("tensor", lambda e: e.matmul(
        pC0[:, 128:256], ones1_f[:], nw_sb[0:1, :], start=True, stop=True), "mm")
    r.wait("vector", "mm")
    r.emit("vector", lambda e: e.tensor_copy(nw2_bc[:, 0:128], pC0[:, 128:256]), "ve")
    r.emit("vector", lambda e: e.tensor_copy(nw2_bc[:, 128:256], pC0[:, 128:256]), "ve")
    for j in range(4):
        r.wait("tensor", "ve")
        r.emit("tensor", lambda e, j=j: e.matmul(
            pC0[:, 0:512], ones1_f[:], ds_row[0:1, j * 512:(j + 1) * 512],
            start=True, stop=True), "mm")
        r.wait("vector", "mm")
        r.emit("vector", lambda e, j=j: e.tensor_copy(
            ds_bc[:, j * 512:(j + 1) * 512], pC0[:, 0:512]), "ve")

    r.mark = {}
    r.mark['p0'] = r.seq

    for i in range(TPC // 128):
        bi = i % 2
        if i >= 2:
            r.wait("sync", "ve")
        r.emit("sync", lambda e, i=i, bi=bi: e.dma_start(
            out=x0_sb[bi][:], in_=x_sl[128 * i:128 * (i + 1), :]), "lnld", 16)
        r.wait("vector", "lnld")
        if i > 0:
            r.wait("vector", "lnst")
        r.emit("vector", lambda e, bi=bi: e.tensor_reduce(
            st0[:, 0:1], x0_sb[bi][:], AX.X, AL.add, negate=True), "ve")
        r.wait("vector", "ve")
        r.emit("vector", lambda e: e.tensor_scalar(
            st0[:, 1:2], st0[:, 0:1], 1.0 / HID, None, AL.mult), "ve")
        r.wait("scalar", "ve")
        r.emit("scalar", lambda e, bi=bi: e.activation(
            junk0[:], x0_sb[bi][:], AF.Square, bias=st0[:, 1:2],
            accum_out=st0[:, 2:3]), "se")
        r.emit("scalar", lambda e: e.activation(
            st0[:, 3:4], st0[:, 2:3], AF.Sqrt, bias=cst[:, 0:1], scale=1.0 / HID), "se")
        r.wait("vector", "se")
        r.emit("vector", lambda e: e.reciprocal(st0[:, 4:5], st0[:, 3:4]), "ve")
        r.wait("vector", "ve")
        r.emit("vector", lambda e, bi=bi: e.tensor_scalar(
            lnbf_sb[:], x0_sb[bi][:], st0[:, 1:2], st0[:, 4:5], AL.add, AL.mult), "ve")
        r.wait("sync", "ve")
        r.emit("sync", lambda e, i=i: e.dma_start(
            out=ln_sl_d[128 * i:128 * (i + 1), :], in_=lnbf_sb[:]), "lnst", 16)

    r.mark['p0end'] = r.seq
    if hw_debug:
        r.wait("sync", "lnst")
        r.emit("sync", lambda e: e.dma_start(out=hwdump["lnsl"], in_=ln_sl_d), "dbgs", 16)
    r.wait("gpsimd", "lnst")
    r.emit("gpsimd", lambda e: e.collective_compute(
        "AllGather", AL.bypass, replica_groups=[list(range(NCORES))],
        ins=[ln_sl_d.opt()], outs=[ln_full_d.opt()]), "cc")

    if hw_debug:
        r.wait("sync", "cc", 1)
        r.emit("sync", lambda e: e.dma_start(out=hwdump["lnfull"], in_=ln_full_d), "dbgs", 16)
        r.wait("sync", "ve")
        r.emit("sync", lambda e: e.dma_start(out=hwdump["negA"], in_=negA_bc[:]), "dbgs", 16)
        r.emit("sync", lambda e: e.dma_start(out=hwdump["dtbb"], in_=dtb_bc[:]), "dbgs", 16)
        r.emit("sync", lambda e: e.dma_start(out=hwdump["nw2"], in_=nw2_bc[:]), "dbgs", 16)
    r.mark['p12'] = r.seq
    p12 = ExitStack()
    S1 = lambda n, shp, dt: p12.enter_context(nc.sbuf_tensor(n, shp, dt))
    P1 = lambda n, shp, dt=F32: p12.enter_context(nc.psum_tensor(n, shp, dt))
    lnT = [S1(f"lnT{i}", [128, 16 * TB], BF16) for i in range(2)]
    pre_qk = [S1(f"preqk{i}", [128, TB + 3], BF16) for i in range(2)]
    pre_v0 = [S1(f"prev0{i}", [128, TB + 3], BF16) for i in range(2)]
    pre_v1 = [S1(f"prev1{i}", [128, TB + 3], BF16) for i in range(2)]
    qkf = S1("qkf", [128, TB], F32)
    sq2 = S1("sq2", [128, TB], BF16)
    qkn = S1("qkn", [128, TB], BF16)
    kn0 = S1("kn0", [64, TB], BF16)
    v_bf = [S1(f"vbf{i}", [128, TB], BF16) for i in range(2)]
    nrm2 = S1("nrm2", [128, TB], F32)
    nwz_sb = S1("nwz_sb", [128, 256], F32)
    pc4 = S1("pc4", [128, 4], F32)
    gcol = S1("gcol", [128, 2], F32)
    tvm = S1("tvm", [128, 4], F32)
    bc_rows = S1("bc_rows", [1, 512], F32)
    e1_sb = S1("e1_sb", [128, 128], F32)
    df_sb = S1("df_sb", [128, 128], F32)
    mb_sb = S1("mb_sb", [128, 128], F32)
    nt1_sb = S1("nt1_sb", [128, 128], F32)
    nt_bf = S1("nt_bf", [128, 128], BF16)
    nn_bf = S1("nn_bf", [128, 128], BF16)
    ap_bf = S1("ap_bf", [128, 128], BF16)
    a1_sb = S1("a1_sb", [128, 128], F32)
    r_sb = S1("r_sb", [128, 192], BF16)
    y_bf = S1("y_bf", [128, 192], BF16)
    kt_bf = S1("kt_bf", [128, 64], BF16)
    kp_bf = S1("kp_bf", [128, 64], BF16)
    qg_sb = S1("qg_sb", [64, 128], F32)
    qpt_bf = S1("qpt_bf", [64, 128], BF16)
    pt_bf = S1("pt_bf", [64, 64], BF16)
    grow_sb = S1("grow_sb", [1, 128], F32)
    lamr = S1("lamr", [128, 1], F32)
    cl_sb = S1("cl_sb", [128, 1], F32)
    gam_col = S1("gam_col", [128, 1], F32)
    sc1 = S1("sc1", [128, 4], F32)
    osq = S1("osq", [128, 128], F32)
    o_f32 = S1("o_f32", [128, 128], F32)
    s_bf = S1("s_bf", [64, 2 * 128], BF16)
    gated_sb = [S1(f"gated{i}", [128, 256], BF16) for i in range(2)]

    pA = P1("pA", [128, 512])
    pBv = P1("pBv", [128, 512])
    pCv = P1("pCv", [128, 512])
    pY = P1("pY", [128, 192])
    pSq = P1("pSq", [128, 256])
    pTt = P1("pTt", [128, 384], BF16)
    pD = P1("pD", [128, 512])
    pO = pC0

    def tap(nm, src_ap, cond):
        if debug_chunk is None or not cond:
            return
        r.wait("sync", "ve")
        r.wait("sync", "se")
        r.wait("sync", "mm")
        r.emit("sync", lambda e, nm=nm, src_ap=src_ap: e.dma_start(
            out=dbg[nm], in_=src_ap), "dbgs", 16)

    def mm_group(psum_ap, lhsT_fn, rhs_fn, n):
        for t in range(n):
            r.emit("tensor", lambda e, t=t: e.matmul(
                psum_ap, lhsT_fn(t), rhs_fn(t), start=(t == 0), stop=(t == n - 1)),
                "mm" if t == n - 1 else None)

    tbmarks = []
    for b in range(B):
        r.wait("vector", "mm")
        r.emit("vector", lambda e: e.memset(s_bf[:], 0.0), "ve")
        for tb in range(NTB):
            gtb = b * NTB + tb
            bi = gtb % 2
            pb = (gtb + 1) % 2
            tok0 = b * T + tb * TB
            r.wait("sync", "cc", 1)
            if gtb >= 2:
                r.wait("sync", "mm", tbmarks[gtb - 2])
            for dt in range(16):
                r.emit("sync", lambda e, dt=dt, tok0=tok0, bi=bi: e.dma_start(
                    out=lnT[bi][:, dt * TB:(dt + 1) * TB],
                    in_=ln_full_d[tok0:tok0 + TB, dt * 128:(dt + 1) * 128],
                    transpose=True), "lnT", 16)
            r.wait("tensor", "lnT")
            r.wait("tensor", "ve")
            r.wait("tensor", "se")
            mm_group(pA[:, 0:512],
                     lambda t: wqkv_sb[:, 384 * t:384 * t + 128],
                     lambda t, bi=bi: lnT[bi][:, t * TB:(t + 1) * TB], 16)
            mm_group(pBv[:, 0:512],
                     lambda t: wqkv_sb[:, 384 * t + 128:384 * t + 256],
                     lambda t, bi=bi: lnT[bi][:, t * TB:(t + 1) * TB], 16)
            mm_group(pCv[:, 0:512],
                     lambda t: wqkv_sb[:, 384 * t + 256:384 * t + 384],
                     lambda t, bi=bi: lnT[bi][:, t * TB:(t + 1) * TB], 16)
            r.wait("vector", "mm")
            r.wait("scalar", "mm")
            for (dst, src, eng, sm) in ((pre_qk, pA, "vector", "ve"),
                                        (pre_v0, pBv, "scalar", "se"),
                                        (pre_v1, pCv, "scalar", "se")):
                if tb == 0:
                    r.emit("vector", lambda e, dst=dst, bi=bi: e.memset(dst[bi][:, 0:3], 0.0), "ve")
                elif True:
                    r.emit(eng, lambda e, dst=dst, bi=bi, pb=pb: COPY(e,
                        dst[bi][:, 0:3], dst[pb][:, TB:TB + 3]), sm)
                r.emit(eng, lambda e, dst=dst, src=src, bi=bi: COPY(e,
                    dst[bi][:, 3:TB + 3], src[:, 0:512]), sm)
            r.wait("tensor", "ve")
            r.wait("tensor", "se")
            for ct, (pre, psum) in enumerate(((pre_qk, pA), (pre_v0, pBv),
                                              (pre_v1, pCv))):
                for k in range(KS):
                    r.emit("tensor", lambda e, ct=ct, k=k, pre=pre, bi=bi, psum=psum:
                           e.matmul(psum[:, 0:512],
                                    diag_bf[:, (ct * 4 + k) * 128:(ct * 4 + k + 1) * 128],
                                    pre[bi][:, k:k + TB], start=(k == 0), stop=(k == 3)),
                           "mm" if k == 3 else None)
            r.wait("scalar", "mm")
            r.emit("scalar", lambda e: e.activation(qkf[:], pA[:, 0:512], AF.Silu), "se")
            r.emit("scalar", lambda e: e.activation(v_bf[0][:], pBv[:, 0:512], AF.Silu), "se")
            r.emit("scalar", lambda e: e.activation(v_bf[1][:], pCv[:, 0:512], AF.Silu), "se")
            r.wait("vector", "se")
            r.emit("vector", lambda e: e.tensor_mul(sq2[:], qkf[:], qkf[:]), "ve")
            r.wait("tensor", "ve")
            mm_group(pO[:, 0:512], lambda t: blko_bf[:], lambda t: sq2[:], 1)
            r.wait("scalar", "mm")
            r.emit("scalar", lambda e: e.activation(
                nrm2[0:64, :], pO[0:64, 0:512], AF.Sqrt, scale=64.0), "se")
            r.emit("scalar", lambda e: e.activation(
                nrm2[64:128, :], pO[64:128, 0:512], AF.Sqrt, scale=1.0), "se")
            r.wait("vector", "se")
            r.emit("vector", lambda e: e.tensor_scalar(
                nrm2[:], nrm2[:], eps_col[:], None, AL.max), "ve")
            r.wait("vector", "ve")
            r.emit("vector", lambda e: e.reciprocal(nrm2[:], nrm2[:]), "ve")
            r.wait("vector", "ve")
            r.emit("vector", lambda e: e.tensor_mul(qkn[:], qkf[:], nrm2[:]), "ve")
            r.wait("tensor", "ve")
            mm_group(pO[0:64, 0:512], lambda t: sel64_bf[:], lambda t: qkn[:], 1)
            r.wait("scalar", "mm")
            r.emit("scalar", lambda e: e.copy(kn0[:], pO[0:64, 0:512]), "se")
            tcond = debug_chunk is not None and gtb == debug_chunk[0]
            tap("qkn", qkn[:], tcond)
            tap("kn0", kn0[:], tcond)
            tap("v0", v_bf[0][:], tcond)
            tap("v1", v_bf[1][:], tcond)
            for cidx in range(4):
                cs = cidx * 128
                zp = (pBv, pCv, pBv, pCv)[cidx]
                tok0c = tok0 + cs
                gbi = (gtb * 4 + cidx) % 2
                ccond = debug_chunk is not None and (gtb, cidx) == tuple(debug_chunk)
                tap("sin", s_bf[:], ccond)
                r.wait("tensor", "se")
                r.wait("tensor", "ve")
                mm_group(zp[:, 0:260],
                         lambda t, bi=bi, cs=cs: lnT[bi][:, t * TB + cs:t * TB + cs + 128],
                         lambda t: wzba_sb[:, 260 * t:260 * t + 260], 16)
                r.wait("scalar", "mm")
                r.emit("scalar", lambda e, zp=zp: e.activation(
                    nwz_sb[:], zp[:, 0:256], AF.Silu), "se")
                r.wait("vector", "se")
                r.emit("vector", lambda e: e.tensor_mul(nwz_sb[:], nwz_sb[:], nw2_bc[:]), "ve")
                r.emit("scalar", lambda e, zp=zp: e.activation(
                    pc4[:, 0:2], zp[:, 256:258], AF.Sigmoid), "se")
                for vh in range(2):
                    r.emit("vector", lambda e, zp=zp, vh=vh: e.tensor_scalar(
                        tvm[:, vh:vh + 1], zp[:, 258 + vh:259 + vh],
                        dtb_bc[:, vh:vh + 1], 0.0, AL.add, AL.max), "ve")
                    r.emit("vector", lambda e, zp=zp, vh=vh: e.tensor_scalar(
                        tvm[:, 2 + vh:3 + vh], zp[:, 258 + vh:259 + vh],
                        dtb_bc[:, vh:vh + 1], None, AL.add), "ve")
                r.wait("scalar", "ve")
                r.emit("scalar", lambda e: e.activation(tvm[:, 2:4], tvm[:, 2:4], AF.Abs), "se")
                r.emit("scalar", lambda e: e.activation(tvm[:, 2:4], tvm[:, 2:4], AF.Exp, scale=-1.0), "se")
                r.emit("scalar", lambda e: e.activation(tvm[:, 2:4], tvm[:, 2:4], AF.Ln, bias=cst[:, 2:3]), "se")
                r.wait("vector", "se")
                r.emit("vector", lambda e: e.tensor_add(tvm[:, 2:4], tvm[:, 2:4], tvm[:, 0:2]), "ve")
                r.wait("vector", "ve")
                r.emit("vector", lambda e: e.tensor_mul(gcol[:], tvm[:, 2:4], negA_bc[:]), "ve")
                tap("gcol", gcol[:], ccond)
                tap("tvm", tvm[:], ccond)
                r.wait("tensor", "ve")
                mm_group(pO[:, 500:502], lambda t: mube[:], lambda t: gcol[:], 1)
                r.wait("vector", "mm")
                r.emit("vector", lambda e: e.tensor_copy(pc4[:, 2:4], pO[:, 500:502]), "ve")
                r.wait("tensor", "ve")
                for kk in range(4):
                    r.emit("tensor", lambda e, kk=kk: e.transpose(
                        pO[0:1, 128 * kk:128 * (kk + 1)], pc4[:, kk:kk + 1], eye_f[:]),
                        "mm" if kk == 3 else None)
                r.wait("vector", "mm")
                r.emit("vector", lambda e: e.tensor_copy(bc_rows[:], pO[0:1, 0:512]), "ve")
                tap("pc4", pc4[:], ccond)
                tap("bcr", bc_rows[:], ccond)
                r.wait("tensor", "ve")
                r.emit("tensor", lambda e, cs=cs: e.transpose(
                    pTt[:, 128:192], kn0[:, cs:cs + 128], eye_bf[0:64, 0:64]))
                r.emit("tensor", lambda e, cs=cs: e.transpose(
                    pTt[:, 192:320], v_bf[0][:, cs:cs + 128], eye_bf[:]), "mm")
                r.wait("scalar", "mm")
                r.emit("scalar", lambda e: e.copy(kt_bf[:], pTt[:, 128:192]), "se")
                for vh in range(2):
                    r.wait("tensor", "ve")
                    r.wait("tensor", "se")
                    r.emit("tensor", lambda e, cs=cs: e.matmul(
                        pSq[:, 0:128], kn0[:, cs:cs + 128], kn0[:, cs:cs + 128],
                        start=True, stop=True))
                    r.emit("tensor", lambda e, cs=cs: e.matmul(
                        pSq[:, 128:256], kn0[:, cs:cs + 128], qkn[0:64, cs:cs + 128],
                        start=True, stop=True))
                    r.emit("tensor", lambda e, vh=vh: e.matmul(
                        pD[:, 0:128], ones1_f[:], bc_rows[0:1, 128 * (2 + vh):128 * (3 + vh)], start=True, stop=True))
                    r.emit("tensor", lambda e, vh=vh: e.matmul(
                        pD[:, 128:256], ones1_f[:], bc_rows[0:1, 128 * vh:128 * (vh + 1)], start=True, stop=True))
                    r.emit("tensor", lambda e, vh=vh: e.matmul(
                        pD[:, 384:385], ones1_f[:], bc_rows[0:1, 128 * (2 + vh) + 127:128 * (2 + vh) + 128],
                        start=True, stop=True), "mm")
                    r.wait("vector", "mm")
                    r.emit("vector", lambda e: e.tensor_copy(cl_sb[:], pD[:, 384:385]), "ve")
                    r.wait("scalar", "mm")
                    r.wait("scalar", "ve")
                    r.emit("scalar", lambda e, vh=vh: e.activation(
                        grow_sb[:], bc_rows[0:1, 128 * (2 + vh):128 * (3 + vh)], AF.Exp), "se")
                    r.emit("scalar", lambda e, vh=vh: e.activation(
                        gam_col[:], pc4[:, 2 + vh:3 + vh], AF.Exp), "se")
                    r.emit("scalar", lambda e, vh=vh: e.activation(
                        lamr[:], pc4[:, 2 + vh:3 + vh], AF.Exp, scale=-1.0,
                        bias=cl_sb[:]), "se")
                    r.wait("tensor", "se")
                    r.emit("tensor", lambda e: e.matmul(
                        pD[0:64, 256:384], ones1_f[0:1, 0:64], grow_sb[:],
                        start=True, stop=True), "mm")
                    r.wait("vector", "mm")
                    r.emit("vector", lambda e, vh=vh: e.tensor_scalar(
                        e1_sb[:], pD[:, 0:128], pc4[:, 2 + vh:3 + vh], 0.0,
                        AL.subtract, AL.min), "ve")
                    r.wait("scalar", "ve")
                    r.emit("scalar", lambda e: e.activation(df_sb[:], e1_sb[:], AF.Exp), "se")
                    r.emit("vector", lambda e: e.tensor_mul(mb_sb[:], pD[:, 128:256], mubn[:]), "ve")
                    r.wait("vector", "se")
                    r.emit("vector", lambda e: e.tensor_mul(nt1_sb[:], df_sb[:], pSq[:, 0:128]), "ve")
                    r.wait("vector", "ve")
                    r.emit("vector", lambda e: e.tensor_mul(nt_bf[:], nt1_sb[:], mb_sb[:]), "ve")
                    r.emit("vector", lambda e: e.tensor_mul(a1_sb[:], df_sb[:], pSq[:, 128:256]), "ve")
                    r.wait("vector", "ve")
                    r.emit("vector", lambda e: e.tensor_mul(ap_bf[:], a1_sb[:], mube[:]), "ve")
                    tap(f"df{vh}", df_sb[:], ccond)
                    tap(f"nt{vh}", nt_bf[:], ccond)
                    r.wait("tensor", "ve")
                    r.emit("tensor", lambda e: e.transpose(pTt[:, 0:128], nt_bf[:], eye_bf[:]), "mm")
                    r.wait("scalar", "mm")
                    r.emit("scalar", lambda e: e.copy(nn_bf[:], pTt[:, 0:128]), "se")
                    if vh == 1:
                        r.wait("tensor", "se")
                        r.wait("tensor", "ve")
                        r.emit("tensor", lambda e, cs=cs: e.transpose(
                            pTt[:, 192:320], v_bf[1][:, cs:cs + 128], eye_bf[:]), "mm")
                    r.wait("vector", "mm")
                    r.wait("vector", "se")
                    r.emit("vector", lambda e, vh=vh: e.tensor_scalar(
                        r_sb[:, 0:128], pTt[:, 192:320], pc4[:, vh:vh + 1], None, AL.mult), "ve")
                    r.emit("vector", lambda e, vh=vh: e.tensor_scalar(
                        r_sb[:, 128:192], kt_bf[:], pc4[:, vh:vh + 1], gam_col[:],
                        AL.mult, AL.mult), "ve")
                    r.emit("vector", lambda e: e.tensor_scalar(
                        kp_bf[:], kt_bf[:], lamr[:], None, AL.mult), "ve")
                    tap(f"nn{vh}", nn_bf[:], ccond)
                    tap(f"ap{vh}", ap_bf[:], ccond)
                    tap(f"r{vh}", r_sb[:], ccond)
                    tap(f"kp{vh}", kp_bf[:], ccond)
                    r.wait("tensor", "ve")
                    r.wait("tensor", "se")
                    r.emit("tensor", lambda e: e.matmul(pY[:], eye_bf[:], r_sb[:], start=True, stop=False))
                    r.emit("tensor", lambda e: e.matmul(pY[:], nt_bf[:], r_sb[:], start=False, stop=True), "mm")
                    r.wait("vector", "mm")
                    r.emit("vector", lambda e: e.tensor_copy(y_bf[:], pY[:]), "ve")
                    for lev in range(1, 7):
                        r.emit("tensor", lambda e: e.matmul(pSq[:, 0:128], nt_bf[:], nn_bf[:], start=True, stop=True))
                        r.emit("tensor", lambda e: e.matmul(pSq[:, 128:256], nn_bf[:], nt_bf[:], start=True, stop=True), "mm")
                        r.wait("scalar", "mm")
                        r.emit("scalar", lambda e: e.copy(nn_bf[:], pSq[:, 0:128]), "se")
                        r.emit("scalar", lambda e: e.copy(nt_bf[:], pSq[:, 128:256]), "se")
                        r.wait("tensor", "se")
                        r.wait("tensor", "ve")
                        r.emit("tensor", lambda e: e.matmul(pY[:], nt_bf[:], y_bf[:], start=True, stop=True), "mm")
                        r.wait("vector", "mm")
                        r.wait("vector", "ve")
                        r.emit("vector", lambda e: e.tensor_tensor(y_bf[:], pY[:], y_bf[:], AL.add), "ve")
                    tap(f"yb{vh}", y_bf[:], ccond)
                    r.wait("tensor", "se")
                    r.emit("tensor", lambda e: e.matmul(
                        pSq[0:64, 0:128], y_bf[:, 128:192], ap_bf[:], start=True, stop=True))
                    r.emit("tensor", lambda e: e.matmul(
                        pSq[0:64, 128:192], y_bf[:, 128:192], kp_bf[:], start=True, stop=True), "mm")
                    r.wait("vector", "mm")
                    r.emit("vector", lambda e, cs=cs: e.tensor_mul(
                        qg_sb[:], qkn[0:64, cs:cs + 128], pD[0:64, 256:384]), "ve")
                    r.wait("vector", "ve")
                    r.emit("vector", lambda e: e.tensor_sub(qpt_bf[:], qg_sb[:], pSq[0:64, 0:128]), "ve")
                    r.emit("scalar", lambda e: e.activation(
                        sc1[:, 3:4], cl_sb[:], AF.Exp), "se")
                    r.wait("vector", "se")
                    r.emit("vector", lambda e: e.scalar_tensor_tensor(
                        pt_bf[:], eye_f[0:64, 0:64], sc1[0:64, 3:4], pSq[0:64, 128:192],
                        AL.mult, AL.subtract), "ve")
                    tap(f"qpt{vh}", qpt_bf[:], ccond)
                    tap(f"pt{vh}", pt_bf[:], ccond)
                    r.wait("tensor", "ve")
                    r.emit("tensor", lambda e: e.matmul(
                        pO[:, 0:128], ap_bf[:], y_bf[:, 0:128], start=True, stop=False))
                    r.emit("tensor", lambda e, vh=vh: e.matmul(
                        pO[:, 0:128], qpt_bf[:], s_bf[:, 128 * vh:128 * (vh + 1)],
                        start=False, stop=True))
                    r.emit("tensor", lambda e: e.matmul(
                        pO[0:64, 128:256], kp_bf[:], y_bf[:, 0:128], start=True, stop=False))
                    r.emit("tensor", lambda e, vh=vh: e.matmul(
                        pO[0:64, 128:256], pt_bf[:], s_bf[:, 128 * vh:128 * (vh + 1)],
                        start=False, stop=True), "mm")
                    r.wait("vector", "mm")
                    r.emit("vector", lambda e, vh=vh: e.tensor_copy(
                        s_bf[:, 128 * vh:128 * (vh + 1)], pO[0:64, 128:256]), "ve")
                    r.emit("vector", lambda e: e.tensor_copy(o_f32[:], pO[:, 0:128]), "ve")
                    r.wait("scalar", "ve")
                    r.emit("scalar", lambda e: e.activation(
                        osq[:], o_f32[:], AF.Square, accum_out=sc1[:, 0:1]), "se")
                    r.emit("scalar", lambda e: e.activation(
                        sc1[:, 1:2], sc1[:, 0:1], AF.Sqrt, bias=cst[:, 1:2], scale=1.0 / DV), "se")
                    r.wait("vector", "se")
                    r.emit("vector", lambda e: e.reciprocal(sc1[:, 2:3], sc1[:, 1:2]), "ve")
                    r.wait("vector", "ve")
                    if gtb * 4 + cidx >= 2:
                        r.wait("vector", "gst")
                    r.emit("vector", lambda e, vh=vh, gbi=gbi: e.scalar_tensor_tensor(
                        gated_sb[gbi][:, 128 * vh:128 * (vh + 1)], o_f32[:],
                        sc1[:, 2:3], nwz_sb[:, 128 * vh:128 * (vh + 1)],
                        AL.mult, AL.mult), "ve")
                tap("sout", s_bf[:], ccond)
                r.wait("sync", "ve")
                r.emit("sync", lambda e, tok0c=tok0c, gbi=gbi: e.dma_start(
                    out=g_sl_d[tok0c:tok0c + 128, :], in_=gated_sb[gbi][:]), "gst", 16)
            tbmarks.append(r.cnt["mm"])

    r.mark['p12end'] = r.seq
    if hw_debug:
        r.wait("sync", "gst")
        r.emit("sync", lambda e: e.dma_start(out=hwdump["gsl"], in_=g_sl_d), "dbgs", 16)
    r.wait("gpsimd", "gst")
    r.emit("gpsimd", lambda e: e.collective_compute(
        "AllGather", AL.bypass, replica_groups=[list(range(NCORES))],
        ins=[g_sl_d.opt()], outs=[g_full_d.opt()]), "cc")

    def reshuffle(e, S):
        v = e.value_load(cid_sb[0:1, 0:1], min_val=0, max_val=NCORES - 1)
        off = v * TPC
        for k in range(NCORES):
            e.dma_start(out=g_mine_d[0:TPC, 256 * k:256 * (k + 1)],
                        in_=g_full_d[bass.ds(k * TOKS + off, TPC), 0:256]
                        ).then_inc(S["gmv"], 16)
    r.wait("gpsimd", "cc")
    r.raw("gpsimd", reshuffle)
    r.cnt["gmv"] += 16 * NCORES

    if hw_debug:
        r.wait("sync", "gmv")
        r.emit("sync", lambda e: e.dma_start(out=hwdump["gmine"], in_=g_mine_d), "dbgs", 16)
    r.mark['p3'] = r.seq
    p12.close()
    p0.close()
    p3 = ExitStack()
    S3 = lambda n, shp, dt: p3.enter_context(nc.sbuf_tensor(n, shp, dt))
    P3 = lambda n, shp, dt=F32: p3.enter_context(nc.psum_tensor(n, shp, dt))
    h1T = S3("h1T", [128, 16 * HALF], BF16)
    oac = S3("oac", [128, NTT3 * HID], F32)
    st3 = S3("st3", [128, 8], F32)
    gsl_sb = S3("gsl_sb", [128, 512], F32)

    pPBig = P3("pPBig", [128, 2048])
    pG = P3("pG", [128, 512])
    pU = P3("pU", [128, 512])
    pT3 = P3("pT3", [128, 512], BF16)

    for eng in ENGINES:
        r.wait(eng, "cc")
        r.wait(eng, "mm")
        r.wait(eng, "ve")
        r.wait(eng, "se")
    r.wait("sync", "gmv")

    womarks = []
    gumarks = []
    dwmarks = []
    for hf in range(NHALF):
        htok0 = hf * HALF
        pA3 = ExitStack()
        SA = lambda n, shp, dt: pA3.enter_context(nc.sbuf_tensor(n, shp, dt))
        wo_sb = [SA(f"wo{i}_{hf}", [128, 4 * HID], BF16) for i in range(2)]
        gl_sb = [SA(f"gl{i}_{hf}", [128, 4 * HALF], BF16) for i in range(2)]
        x3_sb = SA(f"x3_{hf}", [128, HID], F32)
        o1_sb = SA(f"o1_{hf}", [128, HID], F32)
        h1bf = SA(f"h1bf_{hf}", [128, HID], BF16)
        for cseg in range(4):
            wb = cseg % 2
            gseg = hf * 4 + cseg
            if gseg >= 2:
                r.wait("gpsimd", "mm", womarks[gseg - 2])
                r.wait("sync", "mm", womarks[gseg - 2])
            if hf + cseg > 0:
                if cseg == 0:
                    for sname in ("ve", "se", "yst"):
                        r.wait("gpsimd", sname)
                        r.wait("sync", sname)
            r.emit("gpsimd", lambda e, cseg=cseg, wb=wb, wo_sb=wo_sb: TLOAD(
                e, wo_sb[wb], woutT_in[512 * cseg:512 * (cseg + 1), :], 4), "p3w", 16)
            for cb in range(4):
                gc = cseg * 4 + cb
                r.emit("sync", lambda e, gc=gc, wb=wb, cb=cb, htok0=htok0, gl_sb=gl_sb:
                       e.dma_start(
                           out=gl_sb[wb][:, cb * HALF:(cb + 1) * HALF],
                           in_=g_mine_d[htok0:htok0 + HALF, gc * 128:(gc + 1) * 128],
                           transpose=True), "p3g", 16)
            r.wait("tensor", "p3w")
            r.wait("tensor", "p3g")
            r.wait("tensor", "se")
            for tt in range(NTT3):
                r.wait("tensor", "ve")
                for ob in range(4):
                    for cb in range(4):
                        r.emit("tensor",
                               lambda e, tt=tt, ob=ob, cb=cb, wb=wb, wo_sb=wo_sb, gl_sb=gl_sb:
                               e.matmul(pPBig[:, ob * 512:(ob + 1) * 512],
                                        gl_sb[wb][:, cb * HALF + tt * 128:cb * HALF + (tt + 1) * 128],
                                        wo_sb[wb][:, cb * HID + ob * 512:cb * HID + (ob + 1) * 512],
                                        start=(cb == 0), stop=(cb == 3)),
                               "mm" if (ob == 3 and cb == 3) else None)
                r.wait("vector", "mm")
                for ob in range(4):
                    if cseg == 0:
                        r.emit("vector", lambda e, tt=tt, ob=ob: e.tensor_copy(
                            oac[:, tt * HID + ob * 512:tt * HID + (ob + 1) * 512],
                            pPBig[:, ob * 512:(ob + 1) * 512]), "ve")
                    else:
                        r.emit("vector", lambda e, tt=tt, ob=ob: e.tensor_add(
                            oac[:, tt * HID + ob * 512:tt * HID + (ob + 1) * 512],
                            oac[:, tt * HID + ob * 512:tt * HID + (ob + 1) * 512],
                            pPBig[:, ob * 512:(ob + 1) * 512]), "ve")
            womarks.append(r.cnt["mm"])
        for tt in range(NTT3):
            gtt = hf * NTT3 + tt
            r.wait("sync", "ve")
            r.emit("sync", lambda e, gtt=gtt, x3_sb=x3_sb: e.dma_start(
                out=x3_sb[:], in_=x_sl[128 * gtt:128 * (gtt + 1), :]), "p3x", 16)
            r.wait("vector", "ve")
            r.emit("vector", lambda e, tt=tt: e.tensor_reduce(
                st3[:, 0:1], oac[:, tt * HID:(tt + 1) * HID], AX.X, AL.add,
                negate=True), "ve")
            r.wait("vector", "ve")
            r.emit("vector", lambda e: e.tensor_scalar(
                st3[:, 1:2], st3[:, 0:1], 1.0 / HID, None, AL.mult), "ve")
            r.wait("scalar", "ve")
            r.emit("scalar", lambda e, tt=tt: e.activation(
                pPBig[:], oac[:, tt * HID:(tt + 1) * HID], AF.Square, bias=st3[:, 1:2],
                accum_out=st3[:, 2:3]), "se")
            r.emit("scalar", lambda e: e.activation(
                st3[:, 3:4], st3[:, 2:3], AF.Sqrt, bias=cst[:, 0:1], scale=1.0 / HID), "se")
            r.wait("vector", "se")
            r.emit("vector", lambda e: e.reciprocal(st3[:, 4:5], st3[:, 3:4]), "ve")
            r.wait("vector", "ve")
            if gtt > 0:
                r.wait("vector", "o1st")
            r.wait("vector", "p3x")
            r.emit("vector", lambda e, tt=tt, o1_sb=o1_sb: e.tensor_scalar(
                o1_sb[:], oac[:, tt * HID:(tt + 1) * HID], st3[:, 1:2], st3[:, 4:5],
                AL.add, AL.mult), "ve")
            r.wait("vector", "ve")
            r.emit("vector", lambda e, o1_sb=o1_sb, x3_sb=x3_sb: e.tensor_add(
                o1_sb[:], o1_sb[:], x3_sb[:]), "ve")
            r.wait("sync", "ve")
            r.emit("sync", lambda e, gtt=gtt, o1_sb=o1_sb: e.dma_start(
                out=out1_d[128 * gtt:128 * (gtt + 1), :], in_=o1_sb[:]), "o1st", 16)
            r.wait("vector", "ve")
            r.emit("vector", lambda e, o1_sb=o1_sb: e.tensor_reduce(
                st3[:, 0:1], o1_sb[:], AX.X, AL.add, negate=True), "ve")
            r.wait("vector", "ve")
            r.emit("vector", lambda e: e.tensor_scalar(
                st3[:, 1:2], st3[:, 0:1], 1.0 / HID, None, AL.mult), "ve")
            r.wait("scalar", "ve")
            r.emit("scalar", lambda e, o1_sb=o1_sb: e.activation(
                pPBig[:], o1_sb[:], AF.Square, bias=st3[:, 1:2], accum_out=st3[:, 2:3]), "se")
            r.emit("scalar", lambda e: e.activation(
                st3[:, 3:4], st3[:, 2:3], AF.Sqrt, bias=cst[:, 0:1], scale=1.0 / HID), "se")
            r.wait("vector", "se")
            r.emit("vector", lambda e: e.reciprocal(st3[:, 4:5], st3[:, 3:4]), "ve")
            r.wait("vector", "ve")
            r.emit("vector", lambda e, o1_sb=o1_sb, h1bf=h1bf: e.tensor_scalar(
                h1bf[:], o1_sb[:], st3[:, 1:2], st3[:, 4:5], AL.add, AL.mult), "ve")
            r.wait("tensor", "ve")
            for dt in range(16):
                r.emit("tensor", lambda e, dt=dt, h1bf=h1bf: e.transpose(
                    pT3[:, (dt % 4) * 128:(dt % 4 + 1) * 128],
                    h1bf[:, dt * 128:(dt + 1) * 128], eye_bf[:]),
                    "mm" if dt % 4 == 3 else None)
                if dt % 4 == 3:
                    r.wait("scalar", "mm")
                    for q in range(4):
                        dq = dt - 3 + q
                        r.emit("scalar", lambda e, dq=dq, tt=tt: e.copy(
                            h1T[:, dq * HALF + tt * 128:dq * HALF + (tt + 1) * 128],
                            pT3[:, (dq % 4) * 128:(dq % 4 + 1) * 128]), "se")
                    r.wait("tensor", "se")
        pA3.close()
        for sname in ("mm", "ve", "se", "o1st"):
            r.wait("gpsimd", sname)
        pB3 = ExitStack()
        SBp = lambda n, shp, dt: pB3.enter_context(nc.sbuf_tensor(n, shp, dt))
        hseg = SBp(f"hseg_{hf}", [128, SEG * HALF], BF16)
        gw_sb = [SBp(f"gw{i}_{hf}", [128, 16 * 128], BF16) for i in range(2)]
        uw_sb = [SBp(f"uw{i}_{hf}", [128, 16 * 128], BF16) for i in range(2)]
        dw_sb = [SBp(f"dw{i}_{hf}", [128, SEG * 512], BF16) for i in range(2)]
        for iseg in range(NSEG):
            for il in range(SEG):
                it = iseg * SEG + il
                ib = it % 2
                gi = hf * NIT + it
                if gi >= 2:
                    r.wait("gpsimd", "mm", gumarks[gi - 2])
                r.emit("gpsimd", lambda e, it=it, ib=ib, gw_sb=gw_sb: TLOAD(
                    e, gw_sb[ib], gwT_in[:, 128 * it:128 * (it + 1)], 16), "p3w", 16)
                r.emit("gpsimd", lambda e, it=it, ib=ib, uw_sb=uw_sb: TLOAD(
                    e, uw_sb[ib], uwT_in[:, 128 * it:128 * (it + 1)], 16), "p3w", 16)
                r.wait("tensor", "p3w")
                for tg in range(2):
                    r.wait("tensor", "ve")
                    for dt in range(16):
                        r.emit("tensor", lambda e, dt=dt, tg=tg, ib=ib, gw_sb=gw_sb: e.matmul(
                            pG[:], gw_sb[ib][:, dt * 128:(dt + 1) * 128],
                            h1T[:, dt * HALF + tg * 512:dt * HALF + (tg + 1) * 512],
                            start=(dt == 0), stop=(dt == 15)))
                    for dt in range(16):
                        r.emit("tensor", lambda e, dt=dt, tg=tg, ib=ib, uw_sb=uw_sb: e.matmul(
                            pU[:], uw_sb[ib][:, dt * 128:(dt + 1) * 128],
                            h1T[:, dt * HALF + tg * 512:dt * HALF + (tg + 1) * 512],
                            start=(dt == 0), stop=(dt == 15)),
                            "mm" if dt == 15 else None)
                    r.wait("scalar", "mm")
                    r.emit("scalar", lambda e, it=it: e.activation(
                        gsl_sb[:], pG[:], AF.Silu, scale=gs_t[:, it:it + 1]), "se")
                    r.wait("vector", "se")
                    r.emit("vector", lambda e, it=it, il=il, tg=tg, hseg=hseg:
                           e.scalar_tensor_tensor(
                               hseg[:, il * HALF + tg * 512:il * HALF + (tg + 1) * 512],
                               pU[:], us_t[:, it:it + 1], gsl_sb[:], AL.mult, AL.mult), "ve")
                gumarks.append(r.cnt["mm"])
            r.wait("tensor", "ve")
            for ob in range(4):
                db = ob % 2
                di = (hf * NSEG + iseg) * 4 + ob
                if di >= 2:
                    r.wait("gpsimd", "mm", dwmarks[di - 2])
                r.emit("gpsimd", lambda e, iseg=iseg, ob=ob, db=db, dw_sb=dw_sb: TLOAD(
                    e, dw_sb[db], dwT_in[128 * SEG * iseg:128 * SEG * (iseg + 1),
                                         512 * ob:512 * (ob + 1)], SEG), "p3w", 16)
                r.wait("tensor", "p3w")
                for tt in range(NTT3):
                    obD = 2 + (tt % 2)
                    r.wait("tensor", "ve")
                    for il in range(SEG):
                        r.emit("tensor", lambda e, il=il, tt=tt, db=db, obD=obD,
                               hseg=hseg, dw_sb=dw_sb: e.matmul(
                                   pPBig[:, obD * 512:(obD + 1) * 512],
                                   hseg[:, il * HALF + tt * 128:il * HALF + (tt + 1) * 128],
                                   dw_sb[db][:, il * 512:(il + 1) * 512],
                                   start=(il == 0), stop=(il == SEG - 1)),
                               "mm" if il == SEG - 1 else None)
                    r.wait("vector", "mm")
                    if iseg == 0:
                        r.emit("vector", lambda e, tt=tt, ob=ob, obD=obD: e.tensor_copy(
                            oac[:, tt * HID + ob * 512:tt * HID + (ob + 1) * 512],
                            pPBig[:, obD * 512:(obD + 1) * 512]), "ve")
                    else:
                        r.emit("vector", lambda e, tt=tt, ob=ob, obD=obD: e.tensor_add(
                            oac[:, tt * HID + ob * 512:tt * HID + (ob + 1) * 512],
                            oac[:, tt * HID + ob * 512:tt * HID + (ob + 1) * 512],
                            pPBig[:, obD * 512:(obD + 1) * 512]), "ve")
                dwmarks.append(r.cnt["mm"])
        pB3.close()
        for sname in ("mm", "ve", "se"):
            r.wait("sync", sname)
        pC3 = ExitStack()
        SCp = lambda n, shp, dt: pC3.enter_context(nc.sbuf_tensor(n, shp, dt))
        o1c = SCp(f"o1c_{hf}", [128, HID], F32)
        y_sb = SCp(f"y_{hf}", [128, HID], F32)
        r.wait("sync", "o1st")
        for tt in range(NTT3):
            gtt = hf * NTT3 + tt
            r.wait("sync", "ve")
            r.emit("sync", lambda e, gtt=gtt, o1c=o1c: e.dma_start(
                out=o1c[:], in_=out1_d[128 * gtt:128 * (gtt + 1), :]), "p3x", 16)
            r.emit("vector", lambda e, tt=tt: e.tensor_mul(
                oac[:, tt * HID:(tt + 1) * HID], oac[:, tt * HID:(tt + 1) * HID],
                ds_bc[:]), "ve")
            r.wait("vector", "ve")
            r.emit("vector", lambda e, tt=tt: e.tensor_reduce(
                st3[:, 0:1], oac[:, tt * HID:(tt + 1) * HID], AX.X, AL.add,
                negate=True), "ve")
            r.wait("vector", "ve")
            r.emit("vector", lambda e: e.tensor_scalar(
                st3[:, 1:2], st3[:, 0:1], 1.0 / HID, None, AL.mult), "ve")
            r.wait("scalar", "ve")
            r.emit("scalar", lambda e, tt=tt: e.activation(
                pPBig[:], oac[:, tt * HID:(tt + 1) * HID], AF.Square, bias=st3[:, 1:2],
                accum_out=st3[:, 2:3]), "se")
            r.emit("scalar", lambda e: e.activation(
                st3[:, 3:4], st3[:, 2:3], AF.Sqrt, bias=cst[:, 0:1], scale=1.0 / HID), "se")
            r.wait("vector", "se")
            r.emit("vector", lambda e: e.reciprocal(st3[:, 4:5], st3[:, 3:4]), "ve")
            r.wait("vector", "ve")
            if gtt > 0:
                r.wait("vector", "yst")
            r.wait("vector", "p3x")
            r.emit("vector", lambda e, tt=tt, y_sb=y_sb: e.tensor_scalar(
                y_sb[:], oac[:, tt * HID:(tt + 1) * HID], st3[:, 1:2], st3[:, 4:5],
                AL.add, AL.mult), "ve")
            r.wait("vector", "ve")
            r.emit("vector", lambda e, y_sb=y_sb, o1c=o1c: e.tensor_add(
                y_sb[:], y_sb[:], o1c[:]), "ve")
            r.wait("sync", "ve")
            r.emit("sync", lambda e, gtt=gtt, y_sb=y_sb: e.dma_start(
                out=y_out[128 * gtt:128 * (gtt + 1), :], in_=y_sb[:]), "yst", 16)
        pC3.close()

    if hw_debug:
        r.wait("sync", "o1st")
        r.emit("sync", lambda e: e.dma_start(out=hwdump["o1"], in_=out1_d), "dbgs", 16)
        r.wait("sync", "dbgs")
    if debug_chunk is not None:
        r.wait("sync", "dbgs")
        for nm in dbg:
            r.emit("sync", lambda e, nm=nm: e.dma_start(out=dbg_out[nm], in_=dbg[nm]),
                   "dbgs", 16)
        r.wait("sync", "dbgs")
    r.wait("sync", "yst")

    S = {}
    sems_ctx = ExitStack()
    for name in r.cnt:
        S[name] = sems_ctx.enter_context(nc.semaphore(name))
    with nc.Block() as block:
        @block.sync
        def _(e):
            for sq, fn in r.streams["sync"]:
                if trunc is None or sq < trunc:
                    fn(e, S)

        @block.tensor
        def _(e):
            for sq, fn in r.streams["tensor"]:
                if trunc is None or sq < trunc:
                    fn(e, S)

        @block.scalar
        def _(e):
            for sq, fn in r.streams["scalar"]:
                if trunc is None or sq < trunc:
                    fn(e, S)

        @block.vector
        def _(e):
            for sq, fn in r.streams["vector"]:
                if trunc is None or sq < trunc:
                    fn(e, S)

        @block.gpsimd
        def _(e):
            for sq, fn in r.streams["gpsimd"]:
                if trunc is None or sq < trunc:
                    fn(e, S)
    mybir.codegen_inst_isa_subclasses(nc)
    nc._rec_marks = dict(r.mark)
    return nc


def make_in_maps(inputs):
    x = np.asarray(inputs["L0_attn_x"], np.float32).reshape(TOKS, HID)
    Win = np.asarray(inputs["L0_attn_in_proj_w"], np.float32)
    convw = np.asarray(inputs["L0_attn_conv_w"], np.float32)
    dtb = np.asarray(inputs["L0_attn_dt_bias"], np.float32)
    Alog = np.asarray(inputs["L0_attn_A_log"], np.float32)
    nw = np.asarray(inputs["L0_attn_norm_weight"], np.float32)
    Wout = np.asarray(inputs["L0_attn_out_proj_w"], np.float32)
    gw = np.asarray(inputs["L0_ffn_gate_w"], np.float32)
    uw = np.asarray(inputs["L0_ffn_up_w"], np.float32)
    dw = np.asarray(inputs["L0_ffn_down_w"], np.float32)
    gs = np.asarray(inputs["L0_ffn_gate_s"], np.float32)
    us = np.asarray(inputs["L0_ffn_up_s"], np.float32)
    ds = np.asarray(inputs["L0_ffn_down_s"], np.float32)

    idx = np.arange(128)
    eye = np.eye(128, dtype=np.float32)
    mubn = np.where(idx[:, None] < idx[None, :], -1.0, 0.0).astype(np.float32)
    mube = np.where(idx[:, None] <= idx[None, :], 1.0, 0.0).astype(np.float32)
    blko = np.zeros((128, 128), np.float32)
    blko[:64, :64] = 1.0
    blko[64:, 64:] = 1.0
    sel64 = np.ascontiguousarray(np.eye(128, dtype=np.float32)[:, 64:])
    woutT = np.ascontiguousarray(Wout.T)
    gwT = np.ascontiguousarray(gw.T)
    uwT = np.ascontiguousarray(uw.T)
    dwT = np.ascontiguousarray(dw.T)
    gs_p = np.ascontiguousarray(gs.reshape(NIT, 128).T)
    us_p = np.ascontiguousarray(us.reshape(NIT, 128).T)
    ds_r = np.ascontiguousarray(ds.reshape(1, HID))

    in_maps = []
    for c in range(NCORES):
        Wq = Win[64 * c:64 * c + 64]
        Wk = Win[KD + 64 * c:KD + 64 * c + 64]
        Wv = Win[2 * KD + 256 * c:2 * KD + 256 * c + 256]
        Wz = Win[CD + 256 * c:CD + 256 * c + 256]
        Wb = Win[CD + VD + 2 * c:CD + VD + 2 * c + 2]
        Wa = Win[CD + VD + NVH + 2 * c:CD + VD + NVH + 2 * c + 2]
        cw = np.concatenate([convw[64 * c:64 * c + 64],
                             convw[KD + 64 * c:KD + 64 * c + 64],
                             convw[2 * KD + 256 * c:2 * KD + 256 * c + 256]], 0)
        in_maps.append({
            "x_sl": np.ascontiguousarray(x[c * TPC:(c + 1) * TPC]),
            "wqkvT": np.ascontiguousarray(np.concatenate([Wq, Wk, Wv], 0).T),
            "wzbaT": np.ascontiguousarray(np.concatenate([Wz, Wb, Wa], 0).T),
            "cw_in": np.ascontiguousarray(
                cw.reshape(3, 128, 4).transpose(1, 0, 2).reshape(128, 12)),
            "tiny_in": np.ascontiguousarray(np.concatenate(
                [dtb[2 * c:2 * c + 2], Alog[2 * c:2 * c + 2]]).reshape(1, 4)),
            "nw_in": np.ascontiguousarray(nw.reshape(1, DV)),
            "eye_in": eye, "mubn_in": mubn, "mube_in": mube,
            "blko_in": blko, "sel64_in": sel64,
            "cid_in": np.array([[c]], np.int32),
            "woutT_in": woutT, "gwT_in": gwT, "uwT_in": uwT, "dwT_in": dwT,
            "gs_in": gs_p, "us_in": us_p, "ds_in": ds_r,
        })
    return in_maps


_nc_cache = None


def kernel(**inputs):
    global _nc_cache
    in_maps = make_in_maps(inputs)
    if _nc_cache is None:
        _nc_cache = build_program()
    res = run_bass_kernel_spmd(_nc_cache, in_maps, core_ids=list(range(NCORES)))
    y = np.concatenate([np.asarray(r["y_out"], np.float32) for r in res.results], 0)
    return y.reshape(B, T, HID)


# version 07 (final)
# speedup vs baseline: 1.1024x; 1.0184x over previous
import numpy as np
from contextlib import ExitStack

import concourse.bass as bass
import concourse.mybir as mybir
from concourse.bass_utils import run_bass_kernel_spmd

F32 = mybir.dt.float32
BF16 = mybir.dt.bfloat16
I32 = mybir.dt.int32
AF = mybir.ActivationFunctionType
AL = mybir.AluOpType
AX = mybir.AxisListType

B, T, HID = 2, 8192, 2048
TOKS = B * T
NCORES = 8
TPC = TOKS // NCORES
NKH, NVH, DK, DV, KS = 8, 16, 64, 128, 4
KD, VD = NKH * DK, NVH * DV
CD = 2 * KD + VD
TB = 512
NTB = T // TB
SCALE = DK ** -0.5
INTER = 5632
NIT = INTER // 128
NSEG = 4
SEG = NIT // NSEG
HALF = 1024
NHALF = TPC // HALF
NTT3 = HALF // 128

ENGINES = ("sync", "tensor", "scalar", "vector", "gpsimd")


class Rec:
    def __init__(self):
        self.streams = {e: [] for e in ENGINES}
        self.cnt = {}
        self.seq = 0

    def _app(self, eng, fn):
        self.streams[eng].append((self.seq, fn))
        self.seq += 1

    def sem(self, name):
        self.cnt[name] = 0

    def emit(self, eng, fn, inc=None, inc_n=1):
        if inc is not None:
            self.cnt[inc] += inc_n
            self._app(eng, lambda e, S, fn=fn, inc=inc, n=inc_n: fn(e).then_inc(S[inc], n))
        else:
            self._app(eng, lambda e, S, fn=fn: fn(e))

    def raw(self, eng, fn):
        self._app(eng, fn)

    def wait(self, eng, sname, val=None):
        v = self.cnt[sname] if val is None else val
        if v > 0:
            self._app(eng, lambda e, S, sname=sname, v=v: e.wait_ge(S[sname], v))


def TLOAD(e, dst_sb, src_ap, a):
    return e.dma_start(out=dst_sb[:].rearrange("p (a c) -> p a c", a=a),
                       in_=src_ap.rearrange("(a p) c -> p a c", p=128))


def COPY(e, out, in_):
    if hasattr(e, "tensor_copy"):
        return e.tensor_copy(out, in_)
    return e.copy(out, in_)


def build_program(debug_chunk=None, trunc=None, hw_debug=False):
    nc = bass.Bass("TRN2", target_bir_lowering=False, num_devices=NCORES,
                   detect_race_conditions=False)
    r = Rec()
    for s in ("ld", "ldg", "lnld", "lnst", "cc", "lnT", "gst", "gmv", "p3w", "p3g",
              "p3x", "o1st", "yst", "mm", "ve", "se", "dbgs"):
        r.sem(s)

    di = lambda n, shp, dt=F32: nc.dram_tensor(n, shp, dt, kind="ExternalInput").ap()
    x_sl = di("x_sl", [TPC, HID])
    wqkvT = di("wqkvT", [HID, 384])
    wzbaT = di("wzbaT", [HID, 260])
    cw_in = di("cw_in", [128, 12])
    tiny_in = di("tiny_in", [1, 4])
    nw_in = di("nw_in", [1, DV])
    eye_in = di("eye_in", [128, 128])
    mubn_in = di("mubn_in", [128, 128])
    mube_in = di("mube_in", [128, 128])
    blko_in = di("blko_in", [128, 128])
    sel64_in = di("sel64_in", [128, 64])
    cid_in = di("cid_in", [1, 1], I32)
    woutT_in = di("woutT_in", [HID, HID])
    gwT_in = di("gwT_in", [HID, INTER])
    uwT_in = di("uwT_in", [HID, INTER])
    dwT_in = di("dwT_in", [INTER, HID])
    gs_in = di("gs_in", [128, NIT])
    us_in = di("us_in", [128, NIT])
    ds_in = di("ds_in", [1, HID])

    y_out = nc.dram_tensor("y_out", [TPC, HID], F32, kind="ExternalOutput").ap()

    ln_sl_d = nc.dram_tensor("ln_sl_d", [TPC, HID], BF16).ap()
    ln_full_d = nc.dram_tensor("ln_full_d", [TOKS, HID], BF16, addr_space="Shared").ap()
    g_sl_d = nc.dram_tensor("g_sl_d", [TOKS, 256], BF16).ap()
    g_full_d = nc.dram_tensor("g_full_d", [NCORES * TOKS, 256], BF16, addr_space="Shared").ap()
    g_mine_d = nc.dram_tensor("g_mine_d", [TPC, HID], BF16).ap()
    out1_d = nc.dram_tensor("out1_d", [TPC, HID], F32).ap()
    hwdump = {}
    if hw_debug:
        for nm, src_shape, dt in (("lnsl", [TPC, HID], BF16), ("gsl", [TOKS, 256], BF16),
                                  ("gmine", [TPC, HID], BF16), ("o1", [TPC, HID], F32),
                                  ("lnfull", [TOKS, HID], BF16), ("negA", [128, 2], F32),
                                  ("dtbb", [128, 2], F32), ("nw2", [128, 256], F32)):
            hwdump[nm] = nc.dram_tensor("hwd_" + nm, src_shape, dt, kind="ExternalOutput").ap()
    dbg = {}
    if debug_chunk is not None:
        for nm, shp, dt in (("qkn", [128, 512], BF16), ("kn0", [64, 512], BF16),
                            ("v0", [128, 512], BF16), ("v1", [128, 512], BF16),
                            ("pc4", [128, 4], F32), ("bcr", [1, 512], F32),
                            ("nt0", [128, 128], BF16), ("nt1", [128, 128], BF16),
                            ("nn0", [128, 128], BF16), ("nn1", [128, 128], BF16),
                            ("ap0", [128, 128], BF16), ("ap1", [128, 128], BF16),
                            ("r0", [128, 192], BF16), ("r1", [128, 192], BF16),
                            ("yb0", [128, 192], BF16), ("yb1", [128, 192], BF16),
                            ("qpt0", [64, 128], BF16), ("qpt1", [64, 128], BF16),
                            ("pt0", [64, 64], BF16), ("pt1", [64, 64], BF16),
                            ("kp0", [128, 64], BF16), ("kp1", [128, 64], BF16),
                            ("sin", [64, 256], BF16), ("sout", [64, 256], BF16),
                            ("df0", [128, 128], F32), ("df1", [128, 128], F32),
                            ("gcol", [128, 2], F32), ("tvm", [128, 4], F32)):
            dbg[nm] = nc.dram_tensor("dbg_" + nm, shp, dt).ap()
        dbg_out = {nm: nc.dram_tensor("dbgo_" + nm, list(dbg[nm].tensor.shape),
                                      dbg[nm].tensor.dtype, kind="ExternalOutput").ap()
                   for nm in dbg}

    ctx = ExitStack()
    SB = lambda n, shp, dt: ctx.enter_context(nc.sbuf_tensor(n, shp, dt))

    eye_f = SB("eye_f", [128, 128], F32)
    eye_bf = SB("eye_bf", [128, 128], BF16)
    mubn = SB("mubn", [128, 128], F32)
    mube = SB("mube", [128, 128], F32)
    ones1_f = SB("ones1_f", [1, 128], F32)
    onesb = SB("onesb", [64, 64], BF16)
    cw_sb = SB("cw_sb", [128, 12], F32)
    diag_bf = SB("diag_bf", [128, 12 * 128], BF16)
    tiny_sb = SB("tiny_sb", [1, 4], F32)
    nw_sb = SB("nw_sb", [1, DV], F32)
    dtb_bc = SB("dtb_bc", [128, 2], F32)
    negA_bc = SB("negA_bc", [128, 2], F32)
    nw2_bc = SB("nw2_bc", [128, 256], F32)
    gs_t = SB("gs_t", [128, NIT], F32)
    us_t = SB("us_t", [128, NIT], F32)
    ds_bc = SB("ds_bc", [128, HID], F32)
    cid_sb = SB("cid_sb", [1, 1], I32)
    cst = SB("cst", [128, 4], F32)
    wqkv_sb = SB("wqkv_sb", [128, 16 * 384], BF16)
    wzba_sb = SB("wzba_sb", [128, 16 * 260], BF16)
    blko_bf = SB("blko_bf", [128, 128], BF16)
    sel64_bf = SB("sel64_bf", [128, 64], BF16)
    eps_col = SB("eps_col", [128, 1], F32)

    pC0 = ctx.enter_context(nc.psum_tensor("pC0", [128, 512], F32))

    p0 = ExitStack()
    S0 = lambda n, shp, dt: p0.enter_context(nc.sbuf_tensor(n, shp, dt))
    ds_row = S0("ds_row", [1, HID], F32)
    x0_sb = [S0(f"x0_{i}", [128, HID], F32) for i in range(2)]
    junk0 = S0("junk0", [128, HID], F32)
    lnbf_sb = S0("lnbf", [128, HID], BF16)
    st0 = S0("st0", [128, 8], F32)

    r.emit("gpsimd", lambda e: e.dma_start(out=eye_bf[:], in_=eye_in), "ldg", 16)
    r.emit("gpsimd", lambda e: e.dma_start(out=blko_bf[:], in_=blko_in), "ldg", 16)
    r.emit("gpsimd", lambda e: e.dma_start(out=sel64_bf[:], in_=sel64_in), "ldg", 16)
    r.emit("gpsimd", lambda e: TLOAD(e, wqkv_sb, wqkvT, 16), "ldg", 16)
    r.emit("gpsimd", lambda e: TLOAD(e, wzba_sb, wzbaT, 16), "ldg", 16)
    for dst, src in ((eye_f, eye_in), (mubn, mubn_in), (mube, mube_in),
                     (cw_sb, cw_in), (tiny_sb, tiny_in), (nw_sb, nw_in),
                     (gs_t, gs_in), (us_t, us_in), (ds_row, ds_in),
                     (cid_sb, cid_in)):
        r.emit("sync", lambda e, dst=dst, src=src: e.dma_start(out=dst[:], in_=src),
               "ld", 16)
    r.emit("vector", lambda e: e.memset(ones1_f[:], 1.0), "ve")
    r.emit("vector", lambda e: e.memset(cst[:, 0:1], 1e-5), "ve")
    r.emit("vector", lambda e: e.memset(cst[:, 1:2], 1e-6), "ve")
    r.emit("vector", lambda e: e.memset(cst[:, 2:3], 1.0), "ve")
    r.emit("vector", lambda e: e.memset(eps_col[0:64, :], 8e-12), "ve")
    r.emit("vector", lambda e: e.memset(eps_col[64:128, :], 1e-12), "ve")
    r.emit("vector", lambda e: e.memset(onesb[:], 1.0), "ve")
    r.wait("vector", "ld")
    r.wait("vector", "ldg")
    r.wait("scalar", "ld")
    r.wait("scalar", "ldg")
    r.wait("tensor", "ld")
    r.wait("tensor", "ldg")
    for j in range(12):
        r.emit("vector", lambda e, j=j: e.tensor_scalar(
            diag_bf[:, j * 128:(j + 1) * 128], eye_f[:], cw_sb[:, j:j + 1], None,
            AL.mult), "ve")
    r.emit("scalar", lambda e: e.activation(
        tiny_sb[0:1, 2:4], tiny_sb[0:1, 2:4], AF.Exp), "se")
    r.wait("tensor", "se")
    r.wait("tensor", "ve")
    r.emit("tensor", lambda e: e.matmul(
        pC0[:, 0:4], ones1_f[:], tiny_sb[0:1, 0:4], start=True, stop=True), "mm")
    r.wait("vector", "mm")
    r.emit("vector", lambda e: e.tensor_copy(dtb_bc[:], pC0[:, 0:2]), "ve")
    r.emit("vector", lambda e: e.tensor_scalar(
        negA_bc[:], pC0[:, 2:4], -1.0, None, AL.mult), "ve")
    r.wait("tensor", "ve")
    r.emit("tensor", lambda e: e.matmul(
        pC0[:, 128:256], ones1_f[:], nw_sb[0:1, :], start=True, stop=True), "mm")
    r.wait("vector", "mm")
    r.emit("vector", lambda e: e.tensor_copy(nw2_bc[:, 0:128], pC0[:, 128:256]), "ve")
    r.emit("vector", lambda e: e.tensor_copy(nw2_bc[:, 128:256], pC0[:, 128:256]), "ve")
    for j in range(4):
        r.wait("tensor", "ve")
        r.emit("tensor", lambda e, j=j: e.matmul(
            pC0[:, 0:512], ones1_f[:], ds_row[0:1, j * 512:(j + 1) * 512],
            start=True, stop=True), "mm")
        r.wait("vector", "mm")
        r.emit("vector", lambda e, j=j: e.tensor_copy(
            ds_bc[:, j * 512:(j + 1) * 512], pC0[:, 0:512]), "ve")

    r.mark = {}
    r.mark['p0'] = r.seq

    for i in range(TPC // 128):
        bi = i % 2
        if i >= 2:
            r.wait("sync", "ve")
        r.emit("sync", lambda e, i=i, bi=bi: e.dma_start(
            out=x0_sb[bi][:], in_=x_sl[128 * i:128 * (i + 1), :]), "lnld", 16)
        r.wait("vector", "lnld")
        if i > 0:
            r.wait("vector", "lnst")
        r.emit("vector", lambda e, bi=bi: e.tensor_reduce(
            st0[:, 0:1], x0_sb[bi][:], AX.X, AL.add, negate=True), "ve")
        r.wait("vector", "ve")
        r.emit("vector", lambda e: e.tensor_scalar(
            st0[:, 1:2], st0[:, 0:1], 1.0 / HID, None, AL.mult), "ve")
        r.wait("scalar", "ve")
        r.emit("scalar", lambda e, bi=bi: e.activation(
            junk0[:], x0_sb[bi][:], AF.Square, bias=st0[:, 1:2],
            accum_out=st0[:, 2:3]), "se")
        r.emit("scalar", lambda e: e.activation(
            st0[:, 3:4], st0[:, 2:3], AF.Sqrt, bias=cst[:, 0:1], scale=1.0 / HID), "se")
        r.wait("vector", "se")
        r.emit("vector", lambda e: e.reciprocal(st0[:, 4:5], st0[:, 3:4]), "ve")
        r.wait("vector", "ve")
        r.emit("vector", lambda e, bi=bi: e.tensor_scalar(
            lnbf_sb[:], x0_sb[bi][:], st0[:, 1:2], st0[:, 4:5], AL.add, AL.mult), "ve")
        r.wait("sync", "ve")
        r.emit("sync", lambda e, i=i: e.dma_start(
            out=ln_sl_d[128 * i:128 * (i + 1), :], in_=lnbf_sb[:]), "lnst", 16)

    r.mark['p0end'] = r.seq
    if hw_debug:
        r.wait("sync", "lnst")
        r.emit("sync", lambda e: e.dma_start(out=hwdump["lnsl"], in_=ln_sl_d), "dbgs", 16)
    r.wait("gpsimd", "lnst")
    r.emit("gpsimd", lambda e: e.collective_compute(
        "AllGather", AL.bypass, replica_groups=[list(range(NCORES))],
        ins=[ln_sl_d.opt()], outs=[ln_full_d.opt()]), "cc")

    if hw_debug:
        r.wait("sync", "cc", 1)
        r.emit("sync", lambda e: e.dma_start(out=hwdump["lnfull"], in_=ln_full_d), "dbgs", 16)
        r.wait("sync", "ve")
        r.emit("sync", lambda e: e.dma_start(out=hwdump["negA"], in_=negA_bc[:]), "dbgs", 16)
        r.emit("sync", lambda e: e.dma_start(out=hwdump["dtbb"], in_=dtb_bc[:]), "dbgs", 16)
        r.emit("sync", lambda e: e.dma_start(out=hwdump["nw2"], in_=nw2_bc[:]), "dbgs", 16)
    r.mark['p12'] = r.seq
    p12 = ExitStack()
    S1 = lambda n, shp, dt: p12.enter_context(nc.sbuf_tensor(n, shp, dt))
    P1 = lambda n, shp, dt=F32: p12.enter_context(nc.psum_tensor(n, shp, dt))
    lnT = [S1(f"lnT{i}", [128, 16 * TB], BF16) for i in range(2)]
    pre_qk = [S1(f"preqk{i}", [128, TB + 3], BF16) for i in range(2)]
    pre_v0 = [S1(f"prev0{i}", [128, TB + 3], BF16) for i in range(2)]
    pre_v1 = [S1(f"prev1{i}", [128, TB + 3], BF16) for i in range(2)]
    qkf = S1("qkf", [128, TB], F32)
    sq2 = S1("sq2", [128, TB], BF16)
    qkn = S1("qkn", [128, TB], BF16)
    kn0 = S1("kn0", [64, TB], BF16)
    v_bf = [S1(f"vbf{i}", [128, TB], BF16) for i in range(2)]
    nrm2 = S1("nrm2", [128, TB], F32)
    nwz_sb = S1("nwz_sb", [128, 256], F32)
    pc4 = S1("pc4", [128, 4], F32)
    gcol = S1("gcol", [128, 2], F32)
    tvm = S1("tvm", [128, 4], F32)
    bc_rows = S1("bc_rows", [1, 512], F32)
    e1_sb = S1("e1_sb", [128, 128], F32)
    df_sb = S1("df_sb", [128, 128], F32)
    mb_sb = S1("mb_sb", [128, 128], F32)
    nt1_sb = S1("nt1_sb", [128, 128], F32)
    nnt_t = S1("nnt_t", [128, 256], BF16)
    nn_bf = nnt_t[:, 0:128]
    nt_bf = nnt_t[:, 128:256]
    ap_bf = S1("ap_bf", [128, 128], BF16)
    a1_sb = S1("a1_sb", [128, 128], F32)
    r_sb = S1("r_sb", [128, 192], BF16)
    y_bf = S1("y_bf", [128, 192], BF16)
    kt_bf = S1("kt_bf", [128, 64], BF16)
    kp_bf = S1("kp_bf", [128, 64], BF16)
    qg_sb = S1("qg_sb", [64, 128], F32)
    qpt_bf = S1("qpt_bf", [64, 128], BF16)
    pt_bf = S1("pt_bf", [64, 64], BF16)
    grow_sb = S1("grow_sb", [1, 128], F32)
    lamr = S1("lamr", [128, 1], F32)
    cl_sb = S1("cl_sb", [128, 1], F32)
    gam_col = S1("gam_col", [128, 1], F32)
    sc1 = S1("sc1", [128, 4], F32)
    osq = S1("osq", [128, 128], F32)
    o_f32 = S1("o_f32", [128, 128], F32)
    s_bf = S1("s_bf", [64, 2 * 128], BF16)
    gated_sb = [S1(f"gated{i}", [128, 256], BF16) for i in range(2)]

    pA = P1("pA", [128, 512])
    pBv = P1("pBv", [128, 512])
    pCv = P1("pCv", [128, 512])
    pY = P1("pY", [128, 192])
    pSq = P1("pSq", [128, 256])
    pTt = P1("pTt", [128, 384], BF16)
    pD = P1("pD", [128, 512])
    pO = pC0

    def tap(nm, src_ap, cond):
        if debug_chunk is None or not cond:
            return
        r.wait("sync", "ve")
        r.wait("sync", "se")
        r.wait("sync", "mm")
        r.emit("sync", lambda e, nm=nm, src_ap=src_ap: e.dma_start(
            out=dbg[nm], in_=src_ap), "dbgs", 16)

    def mm_group(psum_ap, lhsT_fn, rhs_fn, n):
        for t in range(n):
            r.emit("tensor", lambda e, t=t: e.matmul(
                psum_ap, lhsT_fn(t), rhs_fn(t), start=(t == 0), stop=(t == n - 1)),
                "mm" if t == n - 1 else None)

    tbmarks = []
    for b in range(B):
        r.wait("vector", "mm")
        r.emit("vector", lambda e: e.memset(s_bf[:], 0.0), "ve")
        for tb in range(NTB):
            gtb = b * NTB + tb
            bi = gtb % 2
            pb = (gtb + 1) % 2
            tok0 = b * T + tb * TB
            r.wait("sync", "cc", 1)
            if gtb >= 2:
                r.wait("sync", "mm", tbmarks[gtb - 2])
            for dt in range(16):
                r.emit("sync", lambda e, dt=dt, tok0=tok0, bi=bi: e.dma_start(
                    out=lnT[bi][:, dt * TB:(dt + 1) * TB],
                    in_=ln_full_d[tok0:tok0 + TB, dt * 128:(dt + 1) * 128],
                    transpose=True), "lnT", 16)
            r.wait("tensor", "lnT")
            r.wait("tensor", "ve")
            r.wait("tensor", "se")
            mm_group(pA[:, 0:512],
                     lambda t: wqkv_sb[:, 384 * t:384 * t + 128],
                     lambda t, bi=bi: lnT[bi][:, t * TB:(t + 1) * TB], 16)
            mm_group(pBv[:, 0:512],
                     lambda t: wqkv_sb[:, 384 * t + 128:384 * t + 256],
                     lambda t, bi=bi: lnT[bi][:, t * TB:(t + 1) * TB], 16)
            mm_group(pCv[:, 0:512],
                     lambda t: wqkv_sb[:, 384 * t + 256:384 * t + 384],
                     lambda t, bi=bi: lnT[bi][:, t * TB:(t + 1) * TB], 16)
            r.wait("vector", "mm")
            r.wait("scalar", "mm")
            for (dst, src, eng, sm) in ((pre_qk, pA, "vector", "ve"),
                                        (pre_v0, pBv, "scalar", "se"),
                                        (pre_v1, pCv, "scalar", "se")):
                if tb == 0:
                    r.emit("vector", lambda e, dst=dst, bi=bi: e.memset(dst[bi][:, 0:3], 0.0), "ve")
                elif True:
                    r.emit(eng, lambda e, dst=dst, bi=bi, pb=pb: COPY(e,
                        dst[bi][:, 0:3], dst[pb][:, TB:TB + 3]), sm)
                r.emit(eng, lambda e, dst=dst, src=src, bi=bi: COPY(e,
                    dst[bi][:, 3:TB + 3], src[:, 0:512]), sm)
            r.wait("tensor", "ve")
            r.wait("tensor", "se")
            for ct, (pre, psum) in enumerate(((pre_qk, pA), (pre_v0, pBv),
                                              (pre_v1, pCv))):
                for k in range(KS):
                    r.emit("tensor", lambda e, ct=ct, k=k, pre=pre, bi=bi, psum=psum:
                           e.matmul(psum[:, 0:512],
                                    diag_bf[:, (ct * 4 + k) * 128:(ct * 4 + k + 1) * 128],
                                    pre[bi][:, k:k + TB], start=(k == 0), stop=(k == 3)),
                           "mm" if k == 3 else None)
            r.wait("scalar", "mm")
            r.emit("scalar", lambda e: e.activation(qkf[:], pA[:, 0:512], AF.Silu), "se")
            r.emit("scalar", lambda e: e.activation(v_bf[0][:], pBv[:, 0:512], AF.Silu), "se")
            r.emit("scalar", lambda e: e.activation(v_bf[1][:], pCv[:, 0:512], AF.Silu), "se")
            r.wait("vector", "se")
            r.emit("vector", lambda e: e.tensor_mul(sq2[:], qkf[:], qkf[:]), "ve")
            r.wait("tensor", "ve")
            mm_group(pO[:, 0:512], lambda t: blko_bf[:], lambda t: sq2[:], 1)
            r.wait("scalar", "mm")
            r.emit("scalar", lambda e: e.activation(
                nrm2[0:64, :], pO[0:64, 0:512], AF.Sqrt, scale=64.0), "se")
            r.emit("scalar", lambda e: e.activation(
                nrm2[64:128, :], pO[64:128, 0:512], AF.Sqrt, scale=1.0), "se")
            r.wait("vector", "se")
            r.emit("vector", lambda e: e.tensor_scalar(
                nrm2[:], nrm2[:], eps_col[:], None, AL.max), "ve")
            r.wait("vector", "ve")
            r.emit("vector", lambda e: e.reciprocal(nrm2[:], nrm2[:]), "ve")
            r.wait("vector", "ve")
            r.emit("vector", lambda e: e.tensor_mul(qkn[:], qkf[:], nrm2[:]), "ve")
            r.wait("tensor", "ve")
            mm_group(pO[0:64, 0:512], lambda t: sel64_bf[:], lambda t: qkn[:], 1)
            r.wait("scalar", "mm")
            r.emit("scalar", lambda e: e.copy(kn0[:], pO[0:64, 0:512]), "se")
            tcond = debug_chunk is not None and gtb == debug_chunk[0]
            tap("qkn", qkn[:], tcond)
            tap("kn0", kn0[:], tcond)
            tap("v0", v_bf[0][:], tcond)
            tap("v1", v_bf[1][:], tcond)
            for cidx in range(4):
                cs = cidx * 128
                zp = (pBv, pCv, pBv, pCv)[cidx]
                tok0c = tok0 + cs
                gbi = (gtb * 4 + cidx) % 2
                ccond = debug_chunk is not None and (gtb, cidx) == tuple(debug_chunk)
                tap("sin", s_bf[:], ccond)
                r.wait("tensor", "se")
                r.wait("tensor", "ve")
                mm_group(zp[:, 0:260],
                         lambda t, bi=bi, cs=cs: lnT[bi][:, t * TB + cs:t * TB + cs + 128],
                         lambda t: wzba_sb[:, 260 * t:260 * t + 260], 16)
                r.wait("scalar", "mm")
                r.emit("scalar", lambda e, zp=zp: e.activation(
                    nwz_sb[:], zp[:, 0:256], AF.Silu), "se")
                r.wait("vector", "se")
                r.emit("vector", lambda e: e.tensor_mul(nwz_sb[:], nwz_sb[:], nw2_bc[:]), "ve")
                r.emit("scalar", lambda e, zp=zp: e.activation(
                    pc4[:, 0:2], zp[:, 256:258], AF.Sigmoid), "se")
                for vh in range(2):
                    r.emit("vector", lambda e, zp=zp, vh=vh: e.tensor_scalar(
                        tvm[:, vh:vh + 1], zp[:, 258 + vh:259 + vh],
                        dtb_bc[:, vh:vh + 1], 0.0, AL.add, AL.max), "ve")
                    r.emit("vector", lambda e, zp=zp, vh=vh: e.tensor_scalar(
                        tvm[:, 2 + vh:3 + vh], zp[:, 258 + vh:259 + vh],
                        dtb_bc[:, vh:vh + 1], None, AL.add), "ve")
                r.wait("scalar", "ve")
                r.emit("scalar", lambda e: e.activation(tvm[:, 2:4], tvm[:, 2:4], AF.Abs), "se")
                r.emit("scalar", lambda e: e.activation(tvm[:, 2:4], tvm[:, 2:4], AF.Exp, scale=-1.0), "se")
                r.emit("scalar", lambda e: e.activation(tvm[:, 2:4], tvm[:, 2:4], AF.Ln, bias=cst[:, 2:3]), "se")
                r.wait("vector", "se")
                r.emit("vector", lambda e: e.tensor_add(tvm[:, 2:4], tvm[:, 2:4], tvm[:, 0:2]), "ve")
                r.wait("vector", "ve")
                r.emit("vector", lambda e: e.tensor_mul(gcol[:], tvm[:, 2:4], negA_bc[:]), "ve")
                tap("gcol", gcol[:], ccond)
                tap("tvm", tvm[:], ccond)
                r.wait("tensor", "ve")
                mm_group(pO[:, 500:502], lambda t: mube[:], lambda t: gcol[:], 1)
                r.wait("vector", "mm")
                r.emit("vector", lambda e: e.tensor_copy(pc4[:, 2:4], pO[:, 500:502]), "ve")
                r.wait("tensor", "ve")
                for kk in range(4):
                    r.emit("tensor", lambda e, kk=kk: e.transpose(
                        pO[0:1, 128 * kk:128 * (kk + 1)], pc4[:, kk:kk + 1], eye_f[:]),
                        "mm" if kk == 3 else None)
                r.wait("vector", "mm")
                r.emit("vector", lambda e: e.tensor_copy(bc_rows[:], pO[0:1, 0:512]), "ve")
                tap("pc4", pc4[:], ccond)
                tap("bcr", bc_rows[:], ccond)
                r.wait("tensor", "ve")
                r.emit("tensor", lambda e, cs=cs: e.transpose(
                    pTt[:, 128:192], kn0[:, cs:cs + 128], eye_bf[0:64, 0:64]))
                r.emit("tensor", lambda e, cs=cs: e.transpose(
                    pTt[:, 192:320], v_bf[0][:, cs:cs + 128], eye_bf[:]), "mm")
                r.wait("scalar", "mm")
                r.emit("scalar", lambda e: e.copy(kt_bf[:], pTt[:, 128:192]), "se")
                for vh in range(2):
                    r.wait("tensor", "ve")
                    r.wait("tensor", "se")
                    r.emit("tensor", lambda e, cs=cs: e.matmul(
                        pSq[:, 0:128], kn0[:, cs:cs + 128], kn0[:, cs:cs + 128],
                        start=True, stop=True))
                    r.emit("tensor", lambda e, cs=cs: e.matmul(
                        pSq[:, 128:256], kn0[:, cs:cs + 128], qkn[0:64, cs:cs + 128],
                        start=True, stop=True))
                    r.emit("tensor", lambda e, vh=vh: e.matmul(
                        pD[:, 0:128], ones1_f[:], bc_rows[0:1, 128 * (2 + vh):128 * (3 + vh)], start=True, stop=True))
                    r.emit("tensor", lambda e, vh=vh: e.matmul(
                        pD[:, 128:256], ones1_f[:], bc_rows[0:1, 128 * vh:128 * (vh + 1)], start=True, stop=True))
                    r.emit("tensor", lambda e, vh=vh: e.matmul(
                        pD[:, 384:385], ones1_f[:], bc_rows[0:1, 128 * (2 + vh) + 127:128 * (2 + vh) + 128],
                        start=True, stop=True), "mm")
                    r.wait("vector", "mm")
                    r.emit("vector", lambda e: e.tensor_copy(cl_sb[:], pD[:, 384:385]), "ve")
                    r.wait("scalar", "mm")
                    r.wait("scalar", "ve")
                    r.emit("scalar", lambda e, vh=vh: e.activation(
                        grow_sb[:], bc_rows[0:1, 128 * (2 + vh):128 * (3 + vh)], AF.Exp), "se")
                    r.emit("scalar", lambda e, vh=vh: e.activation(
                        gam_col[:], pc4[:, 2 + vh:3 + vh], AF.Exp), "se")
                    r.emit("scalar", lambda e, vh=vh: e.activation(
                        lamr[:], pc4[:, 2 + vh:3 + vh], AF.Exp, scale=-1.0,
                        bias=cl_sb[:]), "se")
                    r.wait("tensor", "se")
                    r.emit("tensor", lambda e: e.matmul(
                        pD[0:64, 256:384], ones1_f[0:1, 0:64], grow_sb[:],
                        start=True, stop=True), "mm")
                    r.wait("vector", "mm")
                    r.emit("vector", lambda e, vh=vh: e.tensor_scalar(
                        e1_sb[:], pD[:, 0:128], pc4[:, 2 + vh:3 + vh], 0.0,
                        AL.subtract, AL.min), "ve")
                    r.wait("scalar", "ve")
                    r.emit("scalar", lambda e: e.activation(df_sb[:], e1_sb[:], AF.Exp), "se")
                    r.emit("vector", lambda e: e.tensor_mul(mb_sb[:], pD[:, 128:256], mubn[:]), "ve")
                    r.wait("vector", "se")
                    r.emit("vector", lambda e: e.tensor_mul(nt1_sb[:], df_sb[:], pSq[:, 0:128]), "ve")
                    r.wait("vector", "ve")
                    r.emit("vector", lambda e: e.tensor_mul(nt_bf, nt1_sb[:], mb_sb[:]), "ve")
                    r.emit("vector", lambda e: e.tensor_mul(a1_sb[:], df_sb[:], pSq[:, 128:256]), "ve")
                    r.wait("vector", "ve")
                    r.emit("vector", lambda e: e.tensor_mul(ap_bf[:], a1_sb[:], mube[:]), "ve")
                    tap(f"df{vh}", df_sb[:], ccond)
                    tap(f"nt{vh}", nt_bf, ccond)
                    r.wait("tensor", "ve")
                    r.emit("tensor", lambda e: e.transpose(pTt[:, 0:128], nt_bf, eye_bf[:]), "mm")
                    r.wait("scalar", "mm")
                    r.emit("scalar", lambda e: e.copy(nn_bf, pTt[:, 0:128]), "se")
                    if vh == 1:
                        r.wait("tensor", "se")
                        r.wait("tensor", "ve")
                        r.emit("tensor", lambda e, cs=cs: e.transpose(
                            pTt[:, 192:320], v_bf[1][:, cs:cs + 128], eye_bf[:]), "mm")
                    r.wait("vector", "mm")
                    r.wait("vector", "se")
                    r.emit("vector", lambda e, vh=vh: e.tensor_scalar(
                        r_sb[:, 0:128], pTt[:, 192:320], pc4[:, vh:vh + 1], None, AL.mult), "ve")
                    r.emit("vector", lambda e, vh=vh: e.tensor_scalar(
                        r_sb[:, 128:192], kt_bf[:], pc4[:, vh:vh + 1], gam_col[:],
                        AL.mult, AL.mult), "ve")
                    r.emit("vector", lambda e: e.tensor_scalar(
                        kp_bf[:], kt_bf[:], lamr[:], None, AL.mult), "ve")
                    tap(f"nn{vh}", nn_bf, ccond)
                    tap(f"ap{vh}", ap_bf[:], ccond)
                    tap(f"r{vh}", r_sb[:], ccond)
                    tap(f"kp{vh}", kp_bf[:], ccond)
                    r.wait("tensor", "ve")
                    r.wait("tensor", "se")
                    r.emit("tensor", lambda e: e.matmul(pY[:], eye_bf[:], r_sb[:], start=True, stop=False))
                    r.emit("tensor", lambda e: e.matmul(pY[:], nt_bf, r_sb[:], start=False, stop=True), "mm")
                    r.wait("vector", "mm")
                    r.emit("vector", lambda e: e.tensor_copy(y_bf[:], pY[:]), "ve")
                    for lev in range(1, 7):
                        r.emit("tensor", lambda e: e.matmul(pSq[:, 0:128], nt_bf, nn_bf, start=True, stop=True))
                        r.emit("tensor", lambda e: e.matmul(pSq[:, 128:256], nn_bf, nt_bf, start=True, stop=True), "mm")
                        r.wait("scalar", "mm")
                        r.emit("scalar", lambda e: e.copy(nnt_t[:], pSq[:, 0:256]), "se")
                        r.wait("tensor", "se")
                        r.wait("tensor", "ve")
                        r.emit("tensor", lambda e: e.matmul(pY[:], nt_bf, y_bf[:], start=True, stop=True), "mm")
                        r.wait("vector", "mm")
                        r.wait("vector", "ve")
                        r.emit("vector", lambda e: e.tensor_tensor(y_bf[:], pY[:], y_bf[:], AL.add), "ve")
                    tap(f"yb{vh}", y_bf[:], ccond)
                    r.wait("tensor", "se")
                    r.emit("tensor", lambda e: e.matmul(
                        pSq[0:64, 0:128], y_bf[:, 128:192], ap_bf[:], start=True, stop=True))
                    r.emit("tensor", lambda e: e.matmul(
                        pSq[0:64, 128:192], y_bf[:, 128:192], kp_bf[:], start=True, stop=True), "mm")
                    r.wait("vector", "mm")
                    r.emit("vector", lambda e, cs=cs: e.tensor_mul(
                        qg_sb[:], qkn[0:64, cs:cs + 128], pD[0:64, 256:384]), "ve")
                    r.wait("vector", "ve")
                    r.emit("vector", lambda e: e.tensor_sub(qpt_bf[:], qg_sb[:], pSq[0:64, 0:128]), "ve")
                    r.emit("scalar", lambda e: e.activation(
                        sc1[:, 3:4], cl_sb[:], AF.Exp), "se")
                    r.wait("vector", "se")
                    r.emit("vector", lambda e: e.scalar_tensor_tensor(
                        pt_bf[:], eye_f[0:64, 0:64], sc1[0:64, 3:4], pSq[0:64, 128:192],
                        AL.mult, AL.subtract), "ve")
                    tap(f"qpt{vh}", qpt_bf[:], ccond)
                    tap(f"pt{vh}", pt_bf[:], ccond)
                    r.wait("tensor", "ve")
                    r.emit("tensor", lambda e: e.matmul(
                        pO[:, 0:128], ap_bf[:], y_bf[:, 0:128], start=True, stop=False))
                    r.emit("tensor", lambda e, vh=vh: e.matmul(
                        pO[:, 0:128], qpt_bf[:], s_bf[:, 128 * vh:128 * (vh + 1)],
                        start=False, stop=True))
                    r.emit("tensor", lambda e: e.matmul(
                        pO[0:64, 128:256], kp_bf[:], y_bf[:, 0:128], start=True, stop=False))
                    r.emit("tensor", lambda e, vh=vh: e.matmul(
                        pO[0:64, 128:256], pt_bf[:], s_bf[:, 128 * vh:128 * (vh + 1)],
                        start=False, stop=True), "mm")
                    r.wait("vector", "mm")
                    r.emit("vector", lambda e, vh=vh: e.tensor_copy(
                        s_bf[:, 128 * vh:128 * (vh + 1)], pO[0:64, 128:256]), "ve")
                    r.emit("vector", lambda e: e.tensor_copy(o_f32[:], pO[:, 0:128]), "ve")
                    r.wait("scalar", "ve")
                    r.emit("scalar", lambda e: e.activation(
                        osq[:], o_f32[:], AF.Square, accum_out=sc1[:, 0:1]), "se")
                    r.emit("scalar", lambda e: e.activation(
                        sc1[:, 1:2], sc1[:, 0:1], AF.Sqrt, bias=cst[:, 1:2], scale=1.0 / DV), "se")
                    r.wait("vector", "se")
                    r.emit("vector", lambda e: e.reciprocal(sc1[:, 2:3], sc1[:, 1:2]), "ve")
                    r.wait("vector", "ve")
                    if gtb * 4 + cidx >= 2:
                        r.wait("vector", "gst")
                    r.emit("vector", lambda e, vh=vh, gbi=gbi: e.scalar_tensor_tensor(
                        gated_sb[gbi][:, 128 * vh:128 * (vh + 1)], o_f32[:],
                        sc1[:, 2:3], nwz_sb[:, 128 * vh:128 * (vh + 1)],
                        AL.mult, AL.mult), "ve")
                tap("sout", s_bf[:], ccond)
                r.wait("sync", "ve")
                r.emit("sync", lambda e, tok0c=tok0c, gbi=gbi: e.dma_start(
                    out=g_sl_d[tok0c:tok0c + 128, :], in_=gated_sb[gbi][:]), "gst", 16)
            tbmarks.append(r.cnt["mm"])

    r.mark['p12end'] = r.seq
    if hw_debug:
        r.wait("sync", "gst")
        r.emit("sync", lambda e: e.dma_start(out=hwdump["gsl"], in_=g_sl_d), "dbgs", 16)
    r.wait("gpsimd", "gst")
    r.emit("gpsimd", lambda e: e.collective_compute(
        "AllGather", AL.bypass, replica_groups=[list(range(NCORES))],
        ins=[g_sl_d.opt()], outs=[g_full_d.opt()]), "cc")

    def reshuffle(e, S):
        v = e.value_load(cid_sb[0:1, 0:1], min_val=0, max_val=NCORES - 1)
        off = v * TPC
        for k in range(NCORES):
            e.dma_start(out=g_mine_d[0:TPC, 256 * k:256 * (k + 1)],
                        in_=g_full_d[bass.ds(k * TOKS + off, TPC), 0:256]
                        ).then_inc(S["gmv"], 16)
    r.wait("gpsimd", "cc")
    r.raw("gpsimd", reshuffle)
    r.cnt["gmv"] += 16 * NCORES

    if hw_debug:
        r.wait("sync", "gmv")
        r.emit("sync", lambda e: e.dma_start(out=hwdump["gmine"], in_=g_mine_d), "dbgs", 16)
    r.mark['p3'] = r.seq
    p12.close()
    p0.close()
    p3 = ExitStack()
    S3 = lambda n, shp, dt: p3.enter_context(nc.sbuf_tensor(n, shp, dt))
    P3 = lambda n, shp, dt=F32: p3.enter_context(nc.psum_tensor(n, shp, dt))
    h1T = S3("h1T", [128, 16 * HALF], BF16)
    oac = S3("oac", [128, NTT3 * HID], F32)
    st3 = S3("st3", [128, 8], F32)
    gsl_sb = S3("gsl_sb", [128, 512], F32)

    pPBig = P3("pPBig", [128, 2048])
    pG = P3("pG", [128, 512])
    pU = P3("pU", [128, 512])
    pT3 = P3("pT3", [128, 512], BF16)

    for eng in ENGINES:
        r.wait(eng, "cc")
        r.wait(eng, "mm")
        r.wait(eng, "ve")
        r.wait(eng, "se")
    r.wait("sync", "gmv")

    womarks = []
    gumarks = []
    dwmarks = []
    for hf in range(NHALF):
        htok0 = hf * HALF
        pA3 = ExitStack()
        SA = lambda n, shp, dt: pA3.enter_context(nc.sbuf_tensor(n, shp, dt))
        wo_sb = [SA(f"wo{i}_{hf}", [128, 4 * HID], BF16) for i in range(2)]
        gl_sb = [SA(f"gl{i}_{hf}", [128, 4 * HALF], BF16) for i in range(2)]
        x3_sb = SA(f"x3_{hf}", [128, HID], F32)
        o1_sb = SA(f"o1_{hf}", [128, HID], F32)
        h1bf = SA(f"h1bf_{hf}", [128, HID], BF16)
        for cseg in range(4):
            wb = cseg % 2
            gseg = hf * 4 + cseg
            if gseg >= 2:
                r.wait("gpsimd", "mm", womarks[gseg - 2])
                r.wait("sync", "mm", womarks[gseg - 2])
            if hf + cseg > 0:
                if cseg == 0:
                    for sname in ("ve", "se", "yst"):
                        r.wait("gpsimd", sname)
                        r.wait("sync", sname)
            r.emit("gpsimd", lambda e, cseg=cseg, wb=wb, wo_sb=wo_sb: TLOAD(
                e, wo_sb[wb], woutT_in[512 * cseg:512 * (cseg + 1), :], 4), "p3w", 16)
            for cb in range(4):
                gc = cseg * 4 + cb
                r.emit("sync", lambda e, gc=gc, wb=wb, cb=cb, htok0=htok0, gl_sb=gl_sb:
                       e.dma_start(
                           out=gl_sb[wb][:, cb * HALF:(cb + 1) * HALF],
                           in_=g_mine_d[htok0:htok0 + HALF, gc * 128:(gc + 1) * 128],
                           transpose=True), "p3g", 16)
            r.wait("tensor", "p3w")
            r.wait("tensor", "p3g")
            r.wait("tensor", "se")
            for tt in range(NTT3):
                r.wait("tensor", "ve")
                for ob in range(4):
                    for cb in range(4):
                        r.emit("tensor",
                               lambda e, tt=tt, ob=ob, cb=cb, wb=wb, wo_sb=wo_sb, gl_sb=gl_sb:
                               e.matmul(pPBig[:, ob * 512:(ob + 1) * 512],
                                        gl_sb[wb][:, cb * HALF + tt * 128:cb * HALF + (tt + 1) * 128],
                                        wo_sb[wb][:, cb * HID + ob * 512:cb * HID + (ob + 1) * 512],
                                        start=(cb == 0), stop=(cb == 3)),
                               "mm" if (ob == 3 and cb == 3) else None)
                r.wait("vector", "mm")
                for ob in range(4):
                    if cseg == 0:
                        r.emit("vector", lambda e, tt=tt, ob=ob: e.tensor_copy(
                            oac[:, tt * HID + ob * 512:tt * HID + (ob + 1) * 512],
                            pPBig[:, ob * 512:(ob + 1) * 512]), "ve")
                    else:
                        r.emit("vector", lambda e, tt=tt, ob=ob: e.tensor_add(
                            oac[:, tt * HID + ob * 512:tt * HID + (ob + 1) * 512],
                            oac[:, tt * HID + ob * 512:tt * HID + (ob + 1) * 512],
                            pPBig[:, ob * 512:(ob + 1) * 512]), "ve")
            womarks.append(r.cnt["mm"])
        for tt in range(NTT3):
            gtt = hf * NTT3 + tt
            r.wait("sync", "ve")
            r.emit("sync", lambda e, gtt=gtt, x3_sb=x3_sb: e.dma_start(
                out=x3_sb[:], in_=x_sl[128 * gtt:128 * (gtt + 1), :]), "p3x", 16)
            r.wait("vector", "ve")
            r.emit("vector", lambda e, tt=tt: e.tensor_reduce(
                st3[:, 0:1], oac[:, tt * HID:(tt + 1) * HID], AX.X, AL.add,
                negate=True), "ve")
            r.wait("vector", "ve")
            r.emit("vector", lambda e: e.tensor_scalar(
                st3[:, 1:2], st3[:, 0:1], 1.0 / HID, None, AL.mult), "ve")
            r.wait("scalar", "ve")
            r.emit("scalar", lambda e, tt=tt: e.activation(
                pPBig[:], oac[:, tt * HID:(tt + 1) * HID], AF.Square, bias=st3[:, 1:2],
                accum_out=st3[:, 2:3]), "se")
            r.emit("scalar", lambda e: e.activation(
                st3[:, 3:4], st3[:, 2:3], AF.Sqrt, bias=cst[:, 0:1], scale=1.0 / HID), "se")
            r.wait("vector", "se")
            r.emit("vector", lambda e: e.reciprocal(st3[:, 4:5], st3[:, 3:4]), "ve")
            r.wait("vector", "ve")
            if gtt > 0:
                r.wait("vector", "o1st")
            r.wait("vector", "p3x")
            r.emit("vector", lambda e, tt=tt, o1_sb=o1_sb: e.tensor_scalar(
                o1_sb[:], oac[:, tt * HID:(tt + 1) * HID], st3[:, 1:2], st3[:, 4:5],
                AL.add, AL.mult), "ve")
            r.wait("vector", "ve")
            r.emit("vector", lambda e, o1_sb=o1_sb, x3_sb=x3_sb: e.tensor_add(
                o1_sb[:], o1_sb[:], x3_sb[:]), "ve")
            r.wait("sync", "ve")
            r.emit("sync", lambda e, gtt=gtt, o1_sb=o1_sb: e.dma_start(
                out=out1_d[128 * gtt:128 * (gtt + 1), :], in_=o1_sb[:]), "o1st", 16)
            r.wait("vector", "ve")
            r.emit("vector", lambda e, o1_sb=o1_sb: e.tensor_reduce(
                st3[:, 0:1], o1_sb[:], AX.X, AL.add, negate=True), "ve")
            r.wait("vector", "ve")
            r.emit("vector", lambda e: e.tensor_scalar(
                st3[:, 1:2], st3[:, 0:1], 1.0 / HID, None, AL.mult), "ve")
            r.wait("scalar", "ve")
            r.emit("scalar", lambda e, o1_sb=o1_sb: e.activation(
                pPBig[:], o1_sb[:], AF.Square, bias=st3[:, 1:2], accum_out=st3[:, 2:3]), "se")
            r.emit("scalar", lambda e: e.activation(
                st3[:, 3:4], st3[:, 2:3], AF.Sqrt, bias=cst[:, 0:1], scale=1.0 / HID), "se")
            r.wait("vector", "se")
            r.emit("vector", lambda e: e.reciprocal(st3[:, 4:5], st3[:, 3:4]), "ve")
            r.wait("vector", "ve")
            r.emit("vector", lambda e, o1_sb=o1_sb, h1bf=h1bf: e.tensor_scalar(
                h1bf[:], o1_sb[:], st3[:, 1:2], st3[:, 4:5], AL.add, AL.mult), "ve")
            r.wait("tensor", "ve")
            for dt in range(16):
                r.emit("tensor", lambda e, dt=dt, h1bf=h1bf: e.transpose(
                    pT3[:, (dt % 4) * 128:(dt % 4 + 1) * 128],
                    h1bf[:, dt * 128:(dt + 1) * 128], eye_bf[:]),
                    "mm" if dt % 4 == 3 else None)
                if dt % 4 == 3:
                    r.wait("scalar", "mm")
                    for q in range(4):
                        dq = dt - 3 + q
                        r.emit("scalar", lambda e, dq=dq, tt=tt: e.copy(
                            h1T[:, dq * HALF + tt * 128:dq * HALF + (tt + 1) * 128],
                            pT3[:, (dq % 4) * 128:(dq % 4 + 1) * 128]), "se")
                    r.wait("tensor", "se")
        pA3.close()
        for sname in ("mm", "ve", "se", "o1st"):
            r.wait("gpsimd", sname)
        pB3 = ExitStack()
        SBp = lambda n, shp, dt: pB3.enter_context(nc.sbuf_tensor(n, shp, dt))
        hseg = SBp(f"hseg_{hf}", [128, SEG * HALF], BF16)
        gw_sb = [SBp(f"gw{i}_{hf}", [128, 16 * 128], BF16) for i in range(2)]
        uw_sb = [SBp(f"uw{i}_{hf}", [128, 16 * 128], BF16) for i in range(2)]
        dw_sb = [SBp(f"dw{i}_{hf}", [128, SEG * 512], BF16) for i in range(2)]
        for iseg in range(NSEG):
            for il in range(SEG):
                it = iseg * SEG + il
                ib = it % 2
                gi = hf * NIT + it
                if gi >= 2:
                    r.wait("gpsimd", "mm", gumarks[gi - 2])
                r.emit("gpsimd", lambda e, it=it, ib=ib, gw_sb=gw_sb: TLOAD(
                    e, gw_sb[ib], gwT_in[:, 128 * it:128 * (it + 1)], 16), "p3w", 16)
                r.emit("gpsimd", lambda e, it=it, ib=ib, uw_sb=uw_sb: TLOAD(
                    e, uw_sb[ib], uwT_in[:, 128 * it:128 * (it + 1)], 16), "p3w", 16)
                r.wait("tensor", "p3w")
                for tg in range(2):
                    r.wait("tensor", "ve")
                    for dt in range(16):
                        r.emit("tensor", lambda e, dt=dt, tg=tg, ib=ib, gw_sb=gw_sb: e.matmul(
                            pG[:], gw_sb[ib][:, dt * 128:(dt + 1) * 128],
                            h1T[:, dt * HALF + tg * 512:dt * HALF + (tg + 1) * 512],
                            start=(dt == 0), stop=(dt == 15)))
                    for dt in range(16):
                        r.emit("tensor", lambda e, dt=dt, tg=tg, ib=ib, uw_sb=uw_sb: e.matmul(
                            pU[:], uw_sb[ib][:, dt * 128:(dt + 1) * 128],
                            h1T[:, dt * HALF + tg * 512:dt * HALF + (tg + 1) * 512],
                            start=(dt == 0), stop=(dt == 15)),
                            "mm" if dt == 15 else None)
                    r.wait("scalar", "mm")
                    r.emit("scalar", lambda e, it=it: e.activation(
                        gsl_sb[:], pG[:], AF.Silu, scale=gs_t[:, it:it + 1]), "se")
                    r.wait("vector", "se")
                    r.emit("vector", lambda e, it=it, il=il, tg=tg, hseg=hseg:
                           e.scalar_tensor_tensor(
                               hseg[:, il * HALF + tg * 512:il * HALF + (tg + 1) * 512],
                               pU[:], us_t[:, it:it + 1], gsl_sb[:], AL.mult, AL.mult), "ve")
                gumarks.append(r.cnt["mm"])
            r.wait("tensor", "ve")
            for ob in range(4):
                db = ob % 2
                di = (hf * NSEG + iseg) * 4 + ob
                if di >= 2:
                    r.wait("gpsimd", "mm", dwmarks[di - 2])
                r.emit("gpsimd", lambda e, iseg=iseg, ob=ob, db=db, dw_sb=dw_sb: TLOAD(
                    e, dw_sb[db], dwT_in[128 * SEG * iseg:128 * SEG * (iseg + 1),
                                         512 * ob:512 * (ob + 1)], SEG), "p3w", 16)
                r.wait("tensor", "p3w")
                for tt in range(NTT3):
                    obD = 2 + (tt % 2)
                    r.wait("tensor", "ve")
                    for il in range(SEG):
                        r.emit("tensor", lambda e, il=il, tt=tt, db=db, obD=obD,
                               hseg=hseg, dw_sb=dw_sb: e.matmul(
                                   pPBig[:, obD * 512:(obD + 1) * 512],
                                   hseg[:, il * HALF + tt * 128:il * HALF + (tt + 1) * 128],
                                   dw_sb[db][:, il * 512:(il + 1) * 512],
                                   start=(il == 0), stop=(il == SEG - 1)),
                               "mm" if il == SEG - 1 else None)
                    r.wait("vector", "mm")
                    if iseg == 0:
                        r.emit("vector", lambda e, tt=tt, ob=ob, obD=obD: e.tensor_copy(
                            oac[:, tt * HID + ob * 512:tt * HID + (ob + 1) * 512],
                            pPBig[:, obD * 512:(obD + 1) * 512]), "ve")
                    else:
                        r.emit("vector", lambda e, tt=tt, ob=ob, obD=obD: e.tensor_add(
                            oac[:, tt * HID + ob * 512:tt * HID + (ob + 1) * 512],
                            oac[:, tt * HID + ob * 512:tt * HID + (ob + 1) * 512],
                            pPBig[:, obD * 512:(obD + 1) * 512]), "ve")
                dwmarks.append(r.cnt["mm"])
        pB3.close()
        for sname in ("mm", "ve", "se"):
            r.wait("sync", sname)
        pC3 = ExitStack()
        SCp = lambda n, shp, dt: pC3.enter_context(nc.sbuf_tensor(n, shp, dt))
        o1c = SCp(f"o1c_{hf}", [128, HID], F32)
        y_sb = SCp(f"y_{hf}", [128, HID], F32)
        r.wait("sync", "o1st")
        for tt in range(NTT3):
            gtt = hf * NTT3 + tt
            r.wait("sync", "ve")
            r.emit("sync", lambda e, gtt=gtt, o1c=o1c: e.dma_start(
                out=o1c[:], in_=out1_d[128 * gtt:128 * (gtt + 1), :]), "p3x", 16)
            r.emit("vector", lambda e, tt=tt: e.tensor_mul(
                oac[:, tt * HID:(tt + 1) * HID], oac[:, tt * HID:(tt + 1) * HID],
                ds_bc[:]), "ve")
            r.wait("vector", "ve")
            r.emit("vector", lambda e, tt=tt: e.tensor_reduce(
                st3[:, 0:1], oac[:, tt * HID:(tt + 1) * HID], AX.X, AL.add,
                negate=True), "ve")
            r.wait("vector", "ve")
            r.emit("vector", lambda e: e.tensor_scalar(
                st3[:, 1:2], st3[:, 0:1], 1.0 / HID, None, AL.mult), "ve")
            r.wait("scalar", "ve")
            r.emit("scalar", lambda e, tt=tt: e.activation(
                pPBig[:], oac[:, tt * HID:(tt + 1) * HID], AF.Square, bias=st3[:, 1:2],
                accum_out=st3[:, 2:3]), "se")
            r.emit("scalar", lambda e: e.activation(
                st3[:, 3:4], st3[:, 2:3], AF.Sqrt, bias=cst[:, 0:1], scale=1.0 / HID), "se")
            r.wait("vector", "se")
            r.emit("vector", lambda e: e.reciprocal(st3[:, 4:5], st3[:, 3:4]), "ve")
            r.wait("vector", "ve")
            if gtt > 0:
                r.wait("vector", "yst")
            r.wait("vector", "p3x")
            r.emit("vector", lambda e, tt=tt, y_sb=y_sb: e.tensor_scalar(
                y_sb[:], oac[:, tt * HID:(tt + 1) * HID], st3[:, 1:2], st3[:, 4:5],
                AL.add, AL.mult), "ve")
            r.wait("vector", "ve")
            r.emit("vector", lambda e, y_sb=y_sb, o1c=o1c: e.tensor_add(
                y_sb[:], y_sb[:], o1c[:]), "ve")
            r.wait("sync", "ve")
            r.emit("sync", lambda e, gtt=gtt, y_sb=y_sb: e.dma_start(
                out=y_out[128 * gtt:128 * (gtt + 1), :], in_=y_sb[:]), "yst", 16)
        pC3.close()

    if hw_debug:
        r.wait("sync", "o1st")
        r.emit("sync", lambda e: e.dma_start(out=hwdump["o1"], in_=out1_d), "dbgs", 16)
        r.wait("sync", "dbgs")
    if debug_chunk is not None:
        r.wait("sync", "dbgs")
        for nm in dbg:
            r.emit("sync", lambda e, nm=nm: e.dma_start(out=dbg_out[nm], in_=dbg[nm]),
                   "dbgs", 16)
        r.wait("sync", "dbgs")
    r.wait("sync", "yst")

    S = {}
    sems_ctx = ExitStack()
    for name in r.cnt:
        S[name] = sems_ctx.enter_context(nc.semaphore(name))
    with nc.Block() as block:
        @block.sync
        def _(e):
            for sq, fn in r.streams["sync"]:
                if trunc is None or sq < trunc:
                    fn(e, S)

        @block.tensor
        def _(e):
            for sq, fn in r.streams["tensor"]:
                if trunc is None or sq < trunc:
                    fn(e, S)

        @block.scalar
        def _(e):
            for sq, fn in r.streams["scalar"]:
                if trunc is None or sq < trunc:
                    fn(e, S)

        @block.vector
        def _(e):
            for sq, fn in r.streams["vector"]:
                if trunc is None or sq < trunc:
                    fn(e, S)

        @block.gpsimd
        def _(e):
            for sq, fn in r.streams["gpsimd"]:
                if trunc is None or sq < trunc:
                    fn(e, S)
    mybir.codegen_inst_isa_subclasses(nc)
    nc._rec_marks = dict(r.mark)
    return nc


def make_in_maps(inputs):
    x = np.asarray(inputs["L0_attn_x"], np.float32).reshape(TOKS, HID)
    Win = np.asarray(inputs["L0_attn_in_proj_w"], np.float32)
    convw = np.asarray(inputs["L0_attn_conv_w"], np.float32)
    dtb = np.asarray(inputs["L0_attn_dt_bias"], np.float32)
    Alog = np.asarray(inputs["L0_attn_A_log"], np.float32)
    nw = np.asarray(inputs["L0_attn_norm_weight"], np.float32)
    Wout = np.asarray(inputs["L0_attn_out_proj_w"], np.float32)
    gw = np.asarray(inputs["L0_ffn_gate_w"], np.float32)
    uw = np.asarray(inputs["L0_ffn_up_w"], np.float32)
    dw = np.asarray(inputs["L0_ffn_down_w"], np.float32)
    gs = np.asarray(inputs["L0_ffn_gate_s"], np.float32)
    us = np.asarray(inputs["L0_ffn_up_s"], np.float32)
    ds = np.asarray(inputs["L0_ffn_down_s"], np.float32)

    idx = np.arange(128)
    eye = np.eye(128, dtype=np.float32)
    mubn = np.where(idx[:, None] < idx[None, :], -1.0, 0.0).astype(np.float32)
    mube = np.where(idx[:, None] <= idx[None, :], 1.0, 0.0).astype(np.float32)
    blko = np.zeros((128, 128), np.float32)
    blko[:64, :64] = 1.0
    blko[64:, 64:] = 1.0
    sel64 = np.ascontiguousarray(np.eye(128, dtype=np.float32)[:, 64:])
    woutT = np.ascontiguousarray(Wout.T)
    gwT = np.ascontiguousarray(gw.T)
    uwT = np.ascontiguousarray(uw.T)
    dwT = np.ascontiguousarray(dw.T)
    gs_p = np.ascontiguousarray(gs.reshape(NIT, 128).T)
    us_p = np.ascontiguousarray(us.reshape(NIT, 128).T)
    ds_r = np.ascontiguousarray(ds.reshape(1, HID))

    in_maps = []
    for c in range(NCORES):
        Wq = Win[64 * c:64 * c + 64]
        Wk = Win[KD + 64 * c:KD + 64 * c + 64]
        Wv = Win[2 * KD + 256 * c:2 * KD + 256 * c + 256]
        Wz = Win[CD + 256 * c:CD + 256 * c + 256]
        Wb = Win[CD + VD + 2 * c:CD + VD + 2 * c + 2]
        Wa = Win[CD + VD + NVH + 2 * c:CD + VD + NVH + 2 * c + 2]
        cw = np.concatenate([convw[64 * c:64 * c + 64],
                             convw[KD + 64 * c:KD + 64 * c + 64],
                             convw[2 * KD + 256 * c:2 * KD + 256 * c + 256]], 0)
        in_maps.append({
            "x_sl": np.ascontiguousarray(x[c * TPC:(c + 1) * TPC]),
            "wqkvT": np.ascontiguousarray(np.concatenate([Wq, Wk, Wv], 0).T),
            "wzbaT": np.ascontiguousarray(np.concatenate([Wz, Wb, Wa], 0).T),
            "cw_in": np.ascontiguousarray(
                cw.reshape(3, 128, 4).transpose(1, 0, 2).reshape(128, 12)),
            "tiny_in": np.ascontiguousarray(np.concatenate(
                [dtb[2 * c:2 * c + 2], Alog[2 * c:2 * c + 2]]).reshape(1, 4)),
            "nw_in": np.ascontiguousarray(nw.reshape(1, DV)),
            "eye_in": eye, "mubn_in": mubn, "mube_in": mube,
            "blko_in": blko, "sel64_in": sel64,
            "cid_in": np.array([[c]], np.int32),
            "woutT_in": woutT, "gwT_in": gwT, "uwT_in": uwT, "dwT_in": dwT,
            "gs_in": gs_p, "us_in": us_p, "ds_in": ds_r,
        })
    return in_maps


_nc_cache = None


def kernel(**inputs):
    global _nc_cache
    in_maps = make_in_maps(inputs)
    if _nc_cache is None:
        _nc_cache = build_program()
    res = run_bass_kernel_spmd(_nc_cache, in_maps, core_ids=list(range(NCORES)))
    y = np.concatenate([np.asarray(r["y_out"], np.float32) for r in res.results], 0)
    return y.reshape(B, T, HID)
